# version 07
# baseline (speedup 1.0000x reference)
import os
import numpy as np
from contextlib import ExitStack

_PH = int(os.environ.get("KPH", "99"))
_KSUB = int(os.environ.get("KSUB", "9"))


class _Cut(Exception):
    pass


import concourse.bass as bass
import concourse.mybir as mybir
from concourse.bass_utils import run_bass_kernel_spmd

F32 = mybir.dt.float32
BF16 = mybir.dt.bfloat16
AX = mybir.AxisListType
ALU = mybir.AluOpType
ACT = mybir.ActivationFunctionType

T = 2048
NT = 16
D = 1024


class Sem:
    def __init__(self, ctx, nc, name):
        self.h = ctx.enter_context(nc.semaphore(name))
        self.n = 0

    def bump(self, k):
        self.n += k
        return self.n


def _rng(ap):
    try:
        name = ap.tensor.name
        lo = ap.offset
        hi = lo + 1
        for step, count in ap.ap:
            hi += abs(step) * (count - 1)
        return (name, lo, hi)
    except Exception:
        return None


def _overlap(aset, bset):
    for a in aset:
        if a is None:
            continue
        for b in bset:
            if b is None:
                continue
            if a[0] == b[0] and a[1] < b[2] and b[1] < a[2]:
                return True
    return False


class KB:
    ENGINES = ("tensor", "vector", "scalar", "gpsimd", "sync")

    def __init__(self, selfsems):
        self.prog = {e: [] for e in self.ENGINES}
        self.guaranteed = {e: {} for e in self.ENGINES}
        self.last = {e: None for e in self.ENGINES}
        self.selfsems = selfsems

    def op(self, eng, f, waits=(), inc=None, inck=1, r=(), w=()):
        reads = [_rng(a) for a in r]
        writes = [_rng(a) for a in w]
        waits = list(waits)
        prev = self.last[eng]
        if prev is not None:
            p_reads, p_writes, p_rec = prev
            if _overlap(p_writes, reads) or _overlap(p_writes, writes)                     or _overlap(p_reads, writes):
                if p_rec[2] is None:
                    ss = self.selfsems[eng]
                    tok = ss.bump(1)
                    p_rec[2] = ss.h
                    p_rec[3] = 1
                    waits.append((ss, tok))
                else:
                    waits.append((p_rec[4], p_rec[5]))
        g = self.guaranteed[eng]
        wl = []
        for s, v in waits:
            if v is None:
                continue
            if v > 0 and g.get(s, -1) < v:
                wl.append((s.h, v))
                g[s] = v
        tok = inc.bump(inck) if inc is not None else None
        rec = [wl, f, inc.h if inc is not None else None, inck,
               inc, tok]
        self.prog[eng].append(rec)
        self.last[eng] = (reads, writes, rec)
        return tok

    def runner(self, eng):
        entries = self.prog[eng]
        def run(e):
            for wl, f, inch, inck, _s, _t in entries:
                for h, v in wl:
                    e.wait_ge(h, v)
                inst = f(e)
                if inch is not None:
                    inst.then_inc(inch, inck)
        return run


def build_program():
    nc = bass.Bass("TRN2", target_bir_lowering=False)

    def din(name, shape):
        return nc.dram_tensor(name, list(shape), F32, kind="ExternalInput").ap()

    x_tm = din("x_tm", (T, D))
    xT_d = din("xT", (D, T))
    wkT_d = din("wkT", (D, D)); wvT_d = din("wvT", (D, D))
    wq_d = din("wq", (D, D)); woT_d = din("woT", (D, D))
    w1a_d = din("w1a", (D, 512)); w1b_d = din("w1b", (D, 512))
    w2_d = din("w2", (512, 64))
    b1c_d = din("b1c", (128, 4)); b2r_d = din("b2r", (1, 64))
    tau_d = din("tau", (1, 1))
    wihT_d = din("wihT", (D, 3072)); whhT_d = din("whhT", (D, 3072))
    bihr_d = din("bihr", (1, 3072)); bhhr_d = din("bhhr", (1, 3072))
    sli_d = din("sli", (64, D)); sliTp_d = din("sliTp", (128, 512))
    wqs_d = din("wqs", (D, 256)); bqr_d = din("bqr", (1, 256))
    wks_d = din("wks", (D, 256)); bkr_d = din("bkr", (1, 256))
    wvs_d = din("wvs", (D, D)); bvr_d = din("bvr", (1, D))
    lngp_d = din("lngp", (128, 8)); lnbp_d = din("lnbp", (128, 8))
    wvp_d = din("wvp", (D, D)); bvpr_d = din("bvpr", (1, D))
    wof_d = din("wof", (D, D)); bor_d = din("bor", (1, D))
    idn_d = din("idn", (128, 128))
    out_hbm = nc.dram_tensor("out", [T, D], F32, kind="ExternalOutput").ap()

    vk_bi = nc.dram_tensor("vk_bi", [128, 1024], F32).ap()
    vk_bo = nc.dram_tensor("vk_bo", [128, 1024], F32).ap()
    si_bi = nc.dram_tensor("si_bi", [64, 1040], F32).ap()
    si_bo = nc.dram_tensor("si_bo", [64, 1040], F32).ap()
    RG = [[0, 1, 2, 3], [4, 5, 6, 7]]

    with ExitStack() as ctx:
        E = ctx.enter_context
        xTc = [E(nc.sbuf_tensor(f"xTc{i}", [128, T], BF16)) for i in range(8)]
        x1bf = E(nc.sbuf_tensor("x1bf", [128, NT * 1024], BF16))
        kvr = E(nc.sbuf_tensor("kvr", [128, 4096], BF16))
        pool = E(nc.sbuf_tensor("pool", [128, 16384], BF16))
        wihb = E(nc.sbuf_tensor("wihb", [128, 12288], BF16))
        gh_bf = E(nc.sbuf_tensor("gh_bf", [64, 3072], BF16))
        fpA = E(nc.sbuf_tensor("fpA", [128, 2048], F32))
        rows = E(nc.sbuf_tensor("rows", [1, 4672], BF16))
        alpha_bf = E(nc.sbuf_tensor("alpha_bf", [128, NT * 64], BF16))
        alphaT = E(nc.sbuf_tensor("alphaT", [128, T], BF16))
        topk = E(nc.sbuf_tensor("topk", [128, 448], F32))
        w2_sb = E(nc.sbuf_tensor("w2_sb", [128, 256], BF16))
        vk_bf = E(nc.sbuf_tensor("vk_bf", [128, 1024], BF16))
        id_bf = E(nc.sbuf_tensor("id_bf", [128, 128], BF16))
        ones_r = E(nc.sbuf_tensor("ones_r", [1, 128], BF16))
        ones64 = E(nc.sbuf_tensor("ones64", [64, 1], BF16))
        ones_c = E(nc.sbuf_tensor("ones_c", [128, 1], BF16))
        mT_f = E(nc.sbuf_tensor("mT_f", [128, 8], F32))
        c_sb = E(nc.sbuf_tensor("c_sb", [128, 4], F32))
        b1_sb = E(nc.sbuf_tensor("b1_sb", [128, 4], F32))
        lng_sb = E(nc.sbuf_tensor("lng_sb", [128, 8], F32))
        lnb_bf = E(nc.sbuf_tensor("lnb_bf", [128, 8], BF16))
        tau_sb = E(nc.sbuf_tensor("tau_sb", [1, 1], F32))
        a_sb = E(nc.sbuf_tensor("a_sb", [1, 1], F32))
        onesf = E(nc.sbuf_tensor("onesf", [1, 128], F32))
        a_b = E(nc.sbuf_tensor("a_b", [128, 1], F32))
        cst = E(nc.sbuf_tensor("cst", [128, 3], F32))
        stats = E(nc.sbuf_tensor("stats", [128, 8], F32))
        bnt = E(nc.sbuf_tensor("bnt", [128, 12], F32))
        sli_bf = E(nc.sbuf_tensor("sli_bf", [64, D], BF16))
        sliT = E(nc.sbuf_tensor("sliT", [128, 512], BF16))
        si_sb = E(nc.sbuf_tensor("si_sb", [64, 1040], F32))
        sin_bf = E(nc.sbuf_tensor("sin_bf", [128, D], BF16))
        siT_bf = E(nc.sbuf_tensor("siT_bf", [128, 512], BF16))
        g_r = E(nc.sbuf_tensor("g_r", [64, D], BF16))
        g_z = E(nc.sbuf_tensor("g_z", [64, D], BF16))
        g_n = E(nc.sbuf_tensor("g_n", [64, D], BF16))
        snew_f = E(nc.sbuf_tensor("snew_f", [64, D], F32))
        snew_bf = E(nc.sbuf_tensor("snew_bf", [128, D], BF16))
        snewT = E(nc.sbuf_tensor("snewT", [128, 512], BF16))
        qT_bf = E(nc.sbuf_tensor("qT_bf", [128, 128], BF16))
        kT_bf = E(nc.sbuf_tensor("kT_bf", [128, 128], BF16))
        v_bf = E(nc.sbuf_tensor("v_bf", [64, D], BF16))
        E_bf = E(nc.sbuf_tensor("E_bf", [128, 64], BF16))
        ET_bf = E(nc.sbuf_tensor("ET_bf", [64, 64], BF16))
        zrow = E(nc.sbuf_tensor("zrow", [64, 4], F32))
        svpp = E(nc.sbuf_tensor("svpp", [128, D], BF16))
        bvppr = E(nc.sbuf_tensor("bvppr", [1, D], BF16))

        psA = E(nc.psum_tensor("psA", [128, 1024], F32))
        psB = E(nc.psum_tensor("psB", [128, 1024], F32))
        psC = E(nc.psum_tensor("psC", [128, 1024], F32))
        psD = E(nc.psum_tensor("psD", [128, 512], F32))
        psT = E(nc.psum_tensor("psT", [128, 512], F32))

        pe = Sem(ctx, nc, "pe")
        ve = Sem(ctx, nc, "ve")
        ae = Sem(ctx, nc, "ae")
        cc = Sem(ctx, nc, "cc")
        vsm = Sem(ctx, nc, "vsm")
        _sems = {}
        def SEMG(name):
            if name not in _sems:
                _sems[name] = Sem(ctx, nc, name)
            return _sems[name]
        s_sml = SEMG("s_sml"); s_wk = SEMG("s_wk")
        s_x0a = SEMG("s_x0a"); s_x0b = SEMG("s_x0b"); s_x0c = SEMG("s_x0c")
        s_wv = SEMG("s_wv"); s_whh = SEMG("s_whh"); s_woT = SEMG("s_woT")
        s_wq = SEMG("s_wq"); s_bih = SEMG("s_bih"); s_wof = SEMG("s_wof")
        s_w1a = SEMG("s_w1a"); s_wih0 = SEMG("s_wih0"); s_w2 = SEMG("s_w2")
        s_wqs = SEMG("s_wqs"); s_wks = SEMG("s_wks"); s_wih1 = SEMG("s_wih1")
        s_wvs = SEMG("s_wvs"); s_wvpA = SEMG("s_wvpA"); s_bvp = SEMG("s_bvp")
        s_wvpB = SEMG("s_wvpB")
        s_a1o = SEMG("s_a1o"); s_a1i = SEMG("s_a1i")
        s_a2o = SEMG("s_a2o"); s_a2i = SEMG("s_a2i")
        s_x = [SEMG("s_xa"), SEMG("s_xb")]
        s_st = [SEMG("s_sta"), SEMG("s_stb")]
        selfsems = {e: SEMG("s_self_" + e) for e in KB.ENGINES}

        kb = KB(selfsems)

        def MM(out, lhsT, rhs, start, stop, waits=()):
            return kb.op("tensor",
                         lambda e, o=out, l=lhsT, r=rhs, s=start, p=stop:
                         e.matmul(o, l, r, start=s, stop=p), waits=waits, inc=pe,
                         r=[lhsT, rhs], w=[out])

        _last_psT = [None]

        def TRN(out, in_, waits=()):
            p = in_.shape[0]
            w = list(waits) + [(s_sml, 208)]
            if _last_psT[0] is not None:
                w.append((ve, _last_psT[0]))
            return kb.op("tensor",
                         lambda e, o=out, i=in_, d=id_bf[0:p, 0:p]:
                         e.matmul(o, i, d, start=True, stop=True),
                         waits=w, inc=pe,
                         r=[in_], w=[out])

        def CPT(out, in_, waits=(), inc=None):
            tok = kb.op("vector", lambda e, o=out, i=in_: e.tensor_copy(o, i),
                        waits=waits, inc=inc, r=[in_], w=[out])
            _last_psT[0] = tok
            return tok

        def CP(eng, out, in_, waits=(), inc=None):
            return kb.op(eng, lambda e, o=out, i=in_: e.tensor_copy(o, i),
                         waits=waits, inc=inc, r=[in_], w=[out])

        def TT(eng, out, a, b, aop, waits=(), inc=None):
            return kb.op(eng, lambda e, o=out, x=a, y=b, p=aop:
                         e.tensor_tensor(o, x, y, p), waits=waits, inc=inc,
                         r=[a, b], w=[out])

        def TS(eng, out, in0, s1, s2, op0, op1=None, waits=(), inc=None):
            rr = [in0] + [x for x in (s1, s2) if hasattr(x, "tensor")]
            if op1 is None:
                return kb.op(eng, lambda e, o=out, i=in0, a=s1, p=op0:
                             e.tensor_scalar(o, i, a, None, p), waits=waits,
                             inc=inc, r=rr, w=[out])
            return kb.op(eng, lambda e, o=out, i=in0, a=s1, b=s2, p=op0, q=op1:
                         e.tensor_scalar(o, i, a, b, p, q), waits=waits,
                         inc=inc, r=rr, w=[out])

        def STT(eng, out, in0, sc, in1, op0, op1, waits=(), inc=None):
            rr = [in0, in1] + [x for x in (sc,) if hasattr(x, "tensor")]
            return kb.op(eng, lambda e, o=out, i=in0, s=sc, j=in1, p=op0, q=op1:
                         e.scalar_tensor_tensor(o, i, s, j, p, q), waits=waits,
                         inc=inc, r=rr, w=[out])

        def AV(out, in_, func, bias=0.0, scale=1.0, waits=()):
            rr = [in_] + [x for x in (bias, scale) if hasattr(x, "tensor")]
            return kb.op("scalar",
                         lambda e, o=out, i=in_, f=func, b=bias, s=scale:
                         e.activation(out=o, in_=i, func=f, bias=b, scale=s),
                         waits=waits, inc=ae, r=rr, w=[out])

        def RED(out, in_, aop, waits=(), inc=None):
            return kb.op("vector", lambda e, o=out, i=in_, p=aop:
                         e.tensor_reduce(o, i, AX.X, p), waits=waits, inc=inc,
                         r=[in_], w=[out])

        def RCP(out, in_, waits=(), inc=None):
            return kb.op("vector", lambda e, o=out, i=in_: e.reciprocal(o, i),
                         waits=waits, inc=inc, r=[in_], w=[out])

        def MSET(eng, ap, val, waits=(), inc=None):
            return kb.op(eng, lambda e, a=ap, v=val: e.memset(a, v),
                         waits=waits, inc=inc, w=[ap])

        def GD(dst, src, sem, waits=()):
            return kb.op("gpsimd", lambda e, d=dst, s=src: e.dma_start(out=d, in_=s),
                         waits=waits, inc=sem, inck=16)

        def SD(dst, src, sem, waits=()):
            return kb.op("scalar", lambda e, d=dst, s=src: e.dma_start(out=d, in_=s),
                         waits=waits, inc=sem, inck=16)

        def BNS(out, in_, waits=(), inc=None):
            return kb.op("vector", lambda e, o=out, i=in_: e.bn_stats(o, i),
                         waits=waits, inc=inc, r=[in_], w=[out])

        def BNA(out, in_, waits=(), inc=None):
            return kb.op("vector", lambda e, o=out, i=in_: e.bn_aggr(o, i),
                         waits=waits, inc=inc, r=[in_], w=[out])

        try:
            for c in range(8):
                GD(xTc[c][:, 0:256], xT_d[c * 128:(c + 1) * 128, 0:256], s_x0a)
            for i in range(8):
                GD(pool[:, i * 1024:(i + 1) * 1024],
                   wkT_d[i * 128:(i + 1) * 128, :], s_wk)
            for i in range(8):
                GD(pool[:, 8192 + i * 1024: 8192 + (i + 1) * 1024],
                   wvT_d[i * 128:(i + 1) * 128, :], s_wv)
            for c in range(8):
                GD(xTc[c][:, 256:1024], xT_d[c * 128:(c + 1) * 128, 256:1024],
                   s_x0b)
            for c in range(8):
                GD(xTc[c][:, 1024:2048], xT_d[c * 128:(c + 1) * 128, 1024:2048],
                   s_x0c)
            for c in range(5):
                GD(x1bf[:, c * 3072:(c + 1) * 3072],
                   whhT_d[c * 128:(c + 1) * 128, :], s_whh)
            for c in range(5, 8):
                GD(wihb[:, (c - 5) * 3072:(c - 4) * 3072],
                   whhT_d[c * 128:(c + 1) * 128, :], s_whh)
            WHH = (s_whh, 128)
            WK = (s_wk, 128)
            WV = (s_wv, 128)
            X0A = (s_x0a, 128)
            X0B = (s_x0b, 128)
            X0C = (s_x0c, 128)
            GD(id_bf[:], idn_d, s_sml)
            GD(sli_bf[:], sli_d, s_sml)
            GD(sliT[:], sliTp_d, s_sml)
            GD(tau_sb[:], tau_d, s_sml)
            GD(rows[0:1, 0:3072], bhhr_d, s_sml)
            GD(rows[0:1, 3072:3136], b2r_d, s_sml)
            GD(rows[0:1, 3136:3392], bqr_d, s_sml)
            GD(rows[0:1, 3392:3648], bkr_d, s_sml)
            GD(rows[0:1, 3648:4672], bvr_d, s_sml)
            GD(lng_sb[:], lngp_d, s_sml)
            GD(lnb_bf[:], lnbp_d, s_sml)
            GD(b1_sb[:], b1c_d, s_sml)
            GD(svpp[64:65, :], bor_d, s_sml)
            SML = (s_sml, 208)
            for f in range(4):
                GD(w2_sb[:, f * 64:(f + 1) * 64],
                   w2_d[f * 128:(f + 1) * 128, :], s_w2)
            s_w1b = [SEMG("s_w1b%d" % w) for w in range(4)]
            for i in range(2):
                SD(fpA[:, 1024 + i * 512:1024 + (i + 1) * 512],
                   w1b_d[i * 128:(i + 1) * 128, :], s_w1b[0])

            MSET("vector", ones_r[:], 1.0)
            MSET("vector", ones64[:], 1.0 / 64.0)
            MSET("vector", ones_c[:], 1.0)
            MSET("vector", onesf[:], 1.0)
            MSET("vector", cst[:, 0:1], 1e-5)
            MSET("vector", si_sb[0:64, 1024:1040], 0.0)
            MSET("vector", sin_bf[64:128, :], 0.0)
            MSET("vector", snew_bf[64:128, :], 0.0)
            MSET("vector", E_bf[64:128, :], 0.0)
            MSET("vector", cst[:, 1:2], 0.125)
            MSET("vector", cst[:, 2:3], 1.0 / 16.0)
            ve_const = MSET("vector", alphaT[64:65, :], 1.0, inc=ve)

            t_abs = AV(a_sb[:], tau_sb[:], ACT.Abs, waits=[SML])
            t_a1 = TS("vector", a_sb[:], a_sb[:], 0.1, None, ALU.add,
                      waits=[(ae, t_abs)], inc=vsm)
            t_a2 = RCP(a_sb[:], a_sb[:], waits=[(vsm, t_a1)], inc=ve)

            kh = [[None, None] for _ in range(NT)]
            vh = [[None, None] for _ in range(NT)]
            k_mm_done = [None] * NT
            v_mm_done = [None] * NT
            vk_add = [None] * NT
            vk_done = None
            t_ab = t_mTe = t_c = t_cpre = None
            c_last = None

            def emit_vk(tp):
                nonlocal vk_done
                w = [(ve, kh[tp][1]), (ve, vh[tp][1])]
                if tp >= 1:
                    w.append((ve, vk_add[tp - 1]))
                for p in range(8):
                    vk_done = MM(psC[:, p * 128:(p + 1) * 128],
                                 kvr[:, 2048 + (tp % 2) * 1024 + p * 128:
                                      2048 + (tp % 2) * 1024 + (p + 1) * 128],
                                 kvr[:, (tp % 2) * 1024 + p * 128:
                                      (tp % 2) * 1024 + (p + 1) * 128],
                                 start=True, stop=True, waits=w)
                    w = ()
                if tp == 0:
                    vk_add[0] = CP("vector", fpA[:, 0:1024], psC[:, 0:1024],
                                   waits=[(pe, vk_done)], inc=ve)
                else:
                    vk_add[tp] = TT("vector", fpA[:, 0:1024], psC[:, 0:1024],
                                    fpA[:, 0:1024], ALU.add,
                                    waits=[(pe, vk_done)], inc=ve)

            for t in range(NT):
                xg = X0A if t < 2 else (X0B if t < 8 else X0C)
                tk = None
                for h in range(2):
                    w = [xg, WK]
                    if t >= 1:
                        w.append((ve, kh[t - 1][h]))
                    for i in range(8):
                        tk = MM(psB[:, h * 512:(h + 1) * 512],
                                xTc[i][:, t * 128:(t + 1) * 128],
                                pool[:, i * 1024 + h * 512:
                                     i * 1024 + h * 512 + 512],
                                start=(i == 0), stop=(i == 7), waits=w)
                        w = ()
                    kh[t][h] = CP("vector",
                                  kvr[:, (t % 2) * 1024 + h * 512:
                                       (t % 2) * 1024 + (h + 1) * 512],
                                  psB[:, h * 512:(h + 1) * 512],
                                  waits=[(pe, tk)], inc=ve)
                k_mm_done[t] = tk
                tv = None
                for h in range(2):
                    w = [xg, WV]
                    if t >= 1:
                        w.append((ve, vh[t - 1][h]))
                    for i in range(8):
                        tv = MM(psA[:, h * 512:(h + 1) * 512],
                                xTc[i][:, t * 128:(t + 1) * 128],
                                pool[:, 8192 + i * 1024 + h * 512:
                                     8192 + i * 1024 + h * 512 + 512],
                                start=(i == 0), stop=(i == 7), waits=w)
                        w = ()
                    vh[t][h] = CP("vector",
                                  kvr[:, 2048 + (t % 2) * 1024 + h * 512:
                                       2048 + (t % 2) * 1024 + (h + 1) * 512],
                                  psA[:, h * 512:(h + 1) * 512],
                                  waits=[(pe, tv)], inc=ve)
                v_mm_done[t] = tv
                if t >= 1:
                    emit_vk(t - 1)
                if t == 5:
                    t_abmm = MM(psD[0:128, 16:17], onesf[:], a_sb[:], True, True,
                                waits=[(ve, t_a2)])
                    t_ab = CP("vector", a_b[:], psD[0:128, 16:17],
                              waits=[(pe, t_abmm)], inc=ve)
                    t_mT = None
                    for c in range(8):
                        t_mT = MM(psD[0:128, c:c + 1],
                                  sli_bf[:, c * 128:(c + 1) * 128], ones64[:],
                                  True, True, waits=[SML])
                    t_mTe = CP("vector", mT_f[:], psD[0:128, 0:8],
                               waits=[(pe, t_mT)], inc=ve)
                if 6 <= t <= 9:
                    wv_ = t - 6
                    for f in range(4):
                        for i in range(2):
                            c_last = MM(psD[0:128, 8 + f:9 + f],
                                        fpA[:, 1024 + i * 512 + f * 128:
                                             1024 + i * 512 + (f + 1) * 128],
                                        mT_f[:, 2 * wv_ + i: 2 * wv_ + i + 1],
                                        start=(wv_ == 0 and i == 0),
                                        stop=(wv_ == 3 and i == 1),
                                        waits=([(s_w1b[wv_], 32), (ve, t_mTe)]
                                               if (f == 0 and i == 0) else ()))
                    if wv_ < 3:
                        for i in range(2):
                            SD(fpA[:, 1024 + i * 512:1024 + (i + 1) * 512],
                               w1b_d[(2 * (wv_ + 1) + i) * 128:
                                     (2 * (wv_ + 1) + i + 1) * 128, :],
                               s_w1b[wv_ + 1],
                               waits=[(pe, c_last)] if i == 0 else ())
                if t == 10:
                    t_cpre = CP("vector", c_sb[:], psD[0:128, 8:12],
                                waits=[(pe, c_last)], inc=ve)
                    t_c = TT("vector", c_sb[:], c_sb[:], b1_sb[:], ALU.add,
                             waits=[SML], inc=ve)
            emit_vk(NT - 1)
            vk_ev = vk_add[NT - 1]

            vk_ev = vk_add[NT - 1]
            kb.op("gpsimd", lambda e: e.dma_start(out=vk_bi, in_=fpA[:, 0:1024]),
                  waits=[(ve, vk_ev)], inc=s_a1o, inck=16)
            t_cc1 = kb.op("gpsimd",
                          lambda e: e.collective_compute("AllReduce", ALU.add,
                                                         replica_groups=RG,
                                                         ins=[vk_bi.opt()],
                                                         outs=[vk_bo.opt()]),
                          waits=[(s_a1o, 16)], inc=cc)
            kb.op("gpsimd", lambda e: e.dma_start(out=vk_bf[:], in_=vk_bo),
                  waits=[(cc, t_cc1)], inc=s_a1i, inck=16)
            if _PH < 22:
                raise _Cut()
            t_z = None
            for p in range(8):
                MSET("vector", vk_bf[0:64, p * 128 + 64: p * 128 + 128], 0.0,
                     waits=[(s_a1i, 16)])
                t_z = MSET("vector", vk_bf[64:128, p * 128: p * 128 + 64], 0.0, inc=ve)

            if _PH < 25:
                raise _Cut()
            gh_ps = [psA, psB, psC]
            gh_done = None
            _NG = 3 if _PH >= 27 else (2 if _PH >= 26 else 1)
            for g in range(_NG):
                for c in range(8):
                    rhs_base = (x1bf if c < 5 else wihb)
                    off = c * 3072 if c < 5 else (c - 5) * 3072
                    for h in range(2):
                        w = []
                        if g == 0 and c == 0 and h == 0:
                            w = [WHH, SML,
                                 (ve, vh[NT - 1][1]), (ve, vk_ev)]
                        gh_done = MM(gh_ps[g][0:64, h * 512:(h + 1) * 512],
                                     sliT[:, c * 64:(c + 1) * 64],
                                     rhs_base[:, off + g * 1024 + h * 512:
                                              off + g * 1024 + h * 512 + 512],
                                     start=(c == 0), stop=False, waits=w)
                for h in range(2):
                    gh_done = MM(gh_ps[g][0:64, h * 512:(h + 1) * 512],
                                 ones_r[0:1, 0:64],
                                 rows[0:1, g * 1024 + h * 512: g * 1024 + h * 512 + 512],
                                 start=False, stop=True, waits=[SML])
            gh_ev = []
            for g in range(_NG):
                gh_ev.append(CP("vector", gh_bf[:, g * 1024:(g + 1) * 1024],
                                gh_ps[g][0:64, 0:1024], waits=[(pe, gh_done)], inc=ve))
            if _PH < 28:
                raise _Cut()
            gh_done_v = gh_ev[-1]

            if _PH < 30:
                raise _Cut()
            for j in range(8):
                GD(pool[:, j * 1024:(j + 1) * 1024],
                   woT_d[j * 128:(j + 1) * 128, :], s_woT,
                   waits=[(pe, k_mm_done[NT - 1])])
            for j in range(8):
                GD(wihb[:, j * 1024:(j + 1) * 1024],
                   wq_d[j * 128:(j + 1) * 128, :], s_wq, waits=[(pe, gh_done)])
            GD(rows[0:1, 0:3072], bihr_d, s_bih)

            m_evs = []
            for j in range(8):
                for h in range(2):
                    k2 = j * 2 + h
                    ps_m = psD if k2 % 2 == 0 else psT
                    w = []
                    if k2 == 0:
                        w = [(ve, t_z), (s_woT, 128), (pe, v_mm_done[NT - 1]),
                             (ve, t_cpre)]
                    if k2 >= 2:
                        w.append((ve, m_evs[k2 - 2]))
                    tmj = MM(ps_m[0:128, 0:512], vk_bf[:, j * 128:(j + 1) * 128],
                             pool[:, j * 1024 + h * 512: j * 1024 + h * 512 + 512],
                             start=True, stop=True, waits=w)
                    m_evs.append(CP("vector",
                                    pool[:, 8192 + j * 1024 + h * 512:
                                         8192 + j * 1024 + h * 512 + 512],
                                    ps_m[0:128, 0:512],
                                    waits=[(pe, tmj)], inc=ve))
            m_last_ev = m_evs[-1]
            n_ev = None
            n_last_mm = None
            n_ev_h = {}
            for i in range(8):
                for h in range(2):
                    for j in range(8):
                        w = []
                        if i == 0 and h == 0 and j == 0:
                            w = [(s_wq, 128), (ve, m_last_ev), (pe, gh_done),
                                 (ve, gh_ev[1])]
                        if j == 0 and i >= 1:
                            w = list(w) + [(ae, n_ev_h[h])]
                        n_last_mm = MM(psB[0:128, h * 512:(h + 1) * 512],
                                       wihb[:, j * 1024 + i * 128:
                                            j * 1024 + (i + 1) * 128],
                                       pool[:, 8192 + j * 1024 + h * 512:
                                            8192 + j * 1024 + h * 512 + 512],
                                       start=(j == 0), stop=(j == 7), waits=w)
                    n_ev = kb.op("scalar",
                                 lambda e, o=pool[:, i * 1024 + h * 512:
                                                  i * 1024 + h * 512 + 512],
                                 p=psB[0:128, h * 512:(h + 1) * 512]:
                                 e.activation(out=o, in_=p, func=ACT.Copy,
                                              bias=0.0, scale=cst[:, 1:2]),
                                 waits=[(pe, n_last_mm)], inc=ae,
                                 r=[psB[0:128, h * 512:(h + 1) * 512]],
                                 w=[pool[:, i * 1024 + h * 512:
                                         i * 1024 + h * 512 + 512]])
                    n_ev_h[h] = n_ev

            for i in range(8):
                GD(kvr[:, i * 512:(i + 1) * 512],
                   w1a_d[i * 128:(i + 1) * 128, :], s_w1a, waits=[(pe, vk_done)])
            for c in range(8):
                GD(wihb[:, c * 1536:(c + 1) * 1536],
                   wihT_d[c * 128:(c + 1) * 128, 0:1536], s_wih0,
                   waits=[(pe, n_last_mm)])

            x_tok = [None] * NT
            y_done = [None] * NT
            x1_done = [None] * NT
            att_done = [None] * NT
            att_h0 = [None] * NT
            tt_add = [None] * NT
            tp_ev = [None] * (NT * 8)
            alpha_done = [None] * NT
            alT_ev = [None] * NT
            lcp = [None] * NT
            lg_t = [None] * NT
            ee = {}
            h_ev = {}
            si_done = None
            lg_done = None

            def load_x(t):
                w = []
                if t == 0:
                    w.append((s_a1o, 16))
                if t >= 2:
                    w.append((ve, y_done[t - 2]))
                x_tok[t] = SD(fpA[:, (t % 2) * 1024:(t % 2) * 1024 + 1024],
                              x_tm[t * 128:(t + 1) * 128, :], s_x[t % 2], waits=w)

            load_x(0)
            load_x(1)

            def emit_transposes(t):
                for c in range(8):
                    w = []
                    if c == 0:
                        w.append((ve, x1_done[t]))
                    tt = TRN(psT[:, (c % 4) * 128:(c % 4) * 128 + 128],
                             x1bf[:, t * 1024 + c * 128: t * 1024 + (c + 1) * 128],
                             waits=w)
                    tp_ev[t * 8 + c] = CPT(xTc[c][:, t * 128:(t + 1) * 128],
                                           psT[:, (c % 4) * 128:(c % 4) * 128 + 128],
                                           waits=[(pe, tt)], inc=ve)

            def emit_h(tb):
                for f in range(4):
                    k = tb * 4 + f
                    w = [(s_w1a, 128), (ve, tp_ev[(4 * tb + 3) * 8 + 7])]
                    if k == 0:
                        w.append((ae, n_ev))
                    if k >= 2:
                        w.append((ae, h_ev[k - 2]))
                    th = None
                    for i in range(8):
                        th = MM(psB[:, (k % 2) * 512:(k % 2) * 512 + 512],
                                kvr[:, i * 512 + f * 128: i * 512 + (f + 1) * 128],
                                xTc[i][:, tb * 512:(tb + 1) * 512],
                                start=(i == 0), stop=(i == 7), waits=w)
                        w = ()
                    h_ev[k] = kb.op("scalar",
                                    lambda e,
                                    o=pool[:, 8192 + (tb % 2) * 2048 + f * 512:
                                           8192 + (tb % 2) * 2048 + (f + 1) * 512],
                                    p=psB[:, (k % 2) * 512:(k % 2) * 512 + 512],
                                    b=c_sb[:, f:f + 1]:
                                    e.activation(out=o, in_=p, func=ACT.Gelu,
                                                 bias=b, scale=1.0),
                                    waits=[(pe, th), (ve, t_c)], inc=ae,
                                    r=[psB[:, (k % 2) * 512:(k % 2) * 512 + 512],
                                       c_sb[:, f:f + 1]],
                                    w=[pool[:, 8192 + (tb % 2) * 2048 + f * 512:
                                            8192 + (tb % 2) * 2048 + (f + 1) * 512]])

            def emit_logits(j):
                tb = j // 4
                reg = psD[0:128, (j % 8) * 64:(j % 8) * 64 + 64]
                w = [(ae, h_ev[tb * 4 + 3]), (s_w2, 64)]
                if j == 0:
                    w.append((ve, m_last_ev))
                if j >= 1:
                    w.append((ve, lcp[j - 1]))
                for f in range(4):
                    MM(reg,
                       pool[:, 8192 + (tb % 2) * 2048 + f * 512 + (j % 4) * 128:
                            8192 + (tb % 2) * 2048 + f * 512 + (j % 4) * 128 + 128],
                       w2_sb[:, f * 64:(f + 1) * 64],
                       start=(f == 0), stop=False, waits=w)
                    w = ()
                lg_t[j] = MM(reg, ones_r[0:1, 0:128], rows[0:1, 3072:3136],
                             start=False, stop=True)

            def emit_chain(j):
                reg = psD[0:128, (j % 8) * 64:(j % 8) * 64 + 64]
                b = (j % 2) * 224
                wv = [(pe, lg_t[j])]
                if j == 0:
                    wv.append((ve, t_ab))
                if j >= 2:
                    wv.append((ae, ee[j - 2]))
                lcp[j] = CP("vector", topk[:, b:b + 64], reg, waits=wv, inc=ve)
                t_m8 = kb.op("vector",
                             lambda e, b=b: e.max(out=topk[:, b + 192:b + 200],
                                                  in_=topk[:, b:b + 64]), inc=vsm,
                             r=[topk[:, b:b + 64]], w=[topk[:, b + 192:b + 200]])
                TS("vector", topk[:, b + 128:b + 192], topk[:, b:b + 64],
                   topk[:, b + 199:b + 200], None, ALU.is_ge, waits=[(vsm, t_m8)])
                t_nm = TS("vector", topk[:, b + 200:b + 201], topk[:, b + 192:b + 193],
                          a_b[:, 0:1], -1.0, ALU.mult, ALU.mult, inc=ve)
                ee[j] = kb.op("scalar",
                              lambda e, b=b: e.activation(out=topk[:, b + 64:b + 128],
                                                          in_=topk[:, b:b + 64],
                                                          func=ACT.Exp,
                                                          bias=topk[:, b + 200:b + 201],
                                                          scale=a_b[:, 0:1]),
                              waits=[(ve, t_nm)], inc=ae,
                              r=[topk[:, b:b + 64], topk[:, b + 200:b + 201],
                                 a_b[:, 0:1]],
                              w=[topk[:, b + 64:b + 128]])
                TT("vector", topk[:, b + 128:b + 192], topk[:, b + 64:b + 128],
                   topk[:, b + 128:b + 192], ALU.mult, waits=[(ae, ee[j])])
                t_red = RED(topk[:, b + 201:b + 202], topk[:, b + 128:b + 192],
                            ALU.add, inc=vsm)
                t_rc = RCP(topk[:, b + 202:b + 203], topk[:, b + 201:b + 202],
                           waits=[(vsm, t_red)], inc=vsm)
                alpha_done[j] = TS("vector", alpha_bf[:, j * 64:(j + 1) * 64],
                                   topk[:, b + 128:b + 192], topk[:, b + 202:b + 203],
                                   None, ALU.mult, waits=[(vsm, t_rc)], inc=ve)

            def emit_alsi(j):
                nonlocal si_done
                ta = TRN(psT[0:64, (j % 4) * 128:(j % 4) * 128 + 128],
                         alpha_bf[:, j * 64:(j + 1) * 64],
                         waits=[(ve, alpha_done[j])])
                alT_ev[j] = CPT(alphaT[0:64, j * 128:(j + 1) * 128],
                                psT[0:64, (j % 4) * 128:(j % 4) * 128 + 128],
                                waits=[(pe, ta)], inc=ve)
                w = [(ve, alpha_done[j])]
                if j == 0:
                    w.append((ve, gh_ev[2]))
                for h in range(2):
                    si_done = MM(psC[0:64, h * 512:(h + 1) * 512],
                                 alpha_bf[:, j * 64:(j + 1) * 64],
                                 x1bf[:, j * 1024 + h * 512: j * 1024 + h * 512 + 512],
                                 start=(j == 0), stop=(j == NT - 1), waits=w)
                    w = ()

            for t in range(NT):
                tk = None
                for h in range(2):
                    w = []
                    if t == 0 and h == 0:
                        w = [(ae, n_ev), (ve, gh_ev[0])]
                    if t >= 1:
                        w = list(w) + [(ve, tt_add[t - 1][h])]
                    for i in range(8):
                        tk = MM(psA[:, h * 512:(h + 1) * 512],
                                xTc[i][:, t * 128:(t + 1) * 128],
                                pool[:, i * 1024 + h * 512: i * 1024 + h * 512 + 512],
                                start=(i == 0), stop=(i == 7), waits=w)
                        w = ()
                    if h == 0:
                        att_h0[t] = tk
                att_done[t] = tk
                xs = fpA[:, (t % 2) * 1024:(t % 2) * 1024 + 1024]
                t0 = TT("vector", xs[:, 0:512], psA[:, 0:512], xs[:, 0:512], ALU.add,
                        waits=[(pe, att_h0[t]), (s_x[t % 2], 16 * (t // 2 + 1))],
                        inc=ve)
                BNS(bnt[:, 0:6], xs[:, 0:512])
                t1 = TT("vector", xs[:, 512:1024], psA[:, 512:1024], xs[:, 512:1024],
                        ALU.add, waits=[(pe, att_done[t])], inc=ve)
                tt_add[t] = (t0, t1)
                BNS(bnt[:, 6:12], xs[:, 512:1024])
                t_bn = BNA(stats[:, 0:2], bnt[:, 0:12], inc=ve)
                t_sq = AV(stats[:, 2:3], stats[:, 1:2], ACT.Sqrt, bias=cst[:, 0:1],
                          scale=1.0, waits=[(ve, t_bn)])
                t_rc = RCP(stats[:, 3:4], stats[:, 2:3], waits=[(ae, t_sq)], inc=vsm)
                x1_done[t] = TS("vector", x1bf[:, t * 1024:(t + 1) * 1024], xs,
                                stats[:, 0:1], stats[:, 3:4], ALU.subtract, ALU.mult,
                                waits=[(vsm, t_rc), (pe, gh_done)], inc=ve)
                y_done[t] = x1_done[t]
                if t + 2 < NT:
                    load_x(t + 2)
                if t >= 1:
                    emit_transposes(t - 1)
                if t >= 5 and (t - 5) % 4 == 0:
                    emit_h((t - 5) // 4)
                if t >= 6:
                    emit_logits(t - 6)
                    emit_chain(t - 6)
                if t >= 7:
                    emit_alsi(t - 7)
            emit_transposes(NT - 1)
            emit_h(3)
            for j in range(10, NT):
                emit_logits(j)
                emit_chain(j)
                emit_alsi(j - 1)
            emit_alsi(NT - 1)
            lg_done = lg_t[NT - 1]

            for i in range(8):
                GD(pool[:, 4352 + i * 256: 4352 + (i + 1) * 256],
                   wqs_d[i * 128:(i + 1) * 128, :], s_wqs,
                   waits=[(pe, att_done[NT - 1])])
            for i in range(8):
                GD(pool[:, 8192 + i * 1024: 8192 + (i + 1) * 1024],
                   wof_d[i * 128:(i + 1) * 128, :], s_wof, waits=[(pe, lg_done)])
            t_den = RED(si_sb[0:64, 1024:1025], alphaT[0:64, :], ALU.add,
                        waits=[(ve, alT_ev[NT - 1])], inc=ve)
            for i in range(8):
                GD(pool[:, i * 256:(i + 1) * 256],
                   wks_d[i * 128:(i + 1) * 128, :], s_wks, waits=[(pe, lg_done)])

            if _PH < 67:
                raise _Cut()
            si_ev = CP("vector", si_sb[0:64, 0:1024], psC[0:64, 0:1024],
                       waits=[(pe, si_done)], inc=ve)
            kb.op("gpsimd", lambda e: e.dma_start(out=si_bi, in_=si_sb[0:64, 0:1040]),
                  waits=[(ve, si_ev), (ve, t_den)], inc=s_a2o, inck=16)
            t_cc2 = kb.op("gpsimd",
                          lambda e: e.collective_compute("AllReduce", ALU.add,
                                                         replica_groups=RG,
                                                         ins=[si_bi.opt()],
                                                         outs=[si_bo.opt()]),
                          waits=[(s_a2o, 16)], inc=cc)
            kb.op("gpsimd", lambda e: e.dma_start(out=si_sb[0:64, 0:1040], in_=si_bo),
                  waits=[(cc, t_cc2)], inc=s_a2i, inck=16)

            if _PH == 67 and _KSUB < 1:
                raise _Cut()
            t_dn = TS("vector", si_sb[0:64, 1025:1026], si_sb[0:64, 1024:1025], 1e-8,
                      None, ALU.add, waits=[(s_a2i, 16)])
            t_dr = RCP(si_sb[0:64, 1026:1027], si_sb[0:64, 1025:1026], inc=vsm)
            t_sin = TS("vector", sin_bf[0:64, :], si_sb[0:64, 0:1024],
                       si_sb[0:64, 1026:1027],
                       None, ALU.mult, waits=[(vsm, t_dr)], inc=ve)

            if _PH == 67 and _KSUB < 2:
                raise _Cut()
            def transpose_64x1024(src_bf, dst, first_wait):
                evs = [None] * 8
                last_trn = None
                for c in range(8):
                    w = []
                    if c == 0:
                        w.append(first_wait)
                    if c >= 4:
                        w.append((ve, evs[c - 4]))
                    last_trn = TRN(psT[0:128, (c % 4) * 128:(c % 4) * 128 + 128],
                                   src_bf[0:128, c * 128:(c + 1) * 128], waits=w)
                    evs[c] = CPT(dst[:, c * 64:(c + 1) * 64],
                                 psT[0:128, (c % 4) * 128:(c % 4) * 128 + 64],
                                 waits=[(pe, last_trn)], inc=ve)
                return last_trn, evs[7]

            _, siT_ev = transpose_64x1024(sin_bf, siT_bf, (ve, t_sin))

            if _PH == 67 and _KSUB < 3:
                raise _Cut()
            giA = None
            for c in range(8):
                for col in range(0, 1536, 512):
                    g = col // 1024
                    h = (col % 1024) // 512
                    w = [(s_wih0, 128)]
                    if c == 0 and col == 0:
                        w += [(ve, siT_ev), (pe, si_done), (ae, h_ev[15])]
                    giA = MM(gh_ps[g][0:64, h * 512:(h + 1) * 512],
                             siT_bf[:, c * 64:(c + 1) * 64],
                             wihb[:, c * 1536 + col: c * 1536 + col + 512],
                             start=(c == 0), stop=False, waits=w)
            for c in range(8):
                GD(wihb[:, c * 1536:(c + 1) * 1536],
                   wihT_d[c * 128:(c + 1) * 128, 1536:3072], s_wih1,
                   waits=[(pe, giA)])
            giB = None
            for c in range(8):
                for col in range(0, 1536, 512):
                    gcol = 1536 + col
                    g = gcol // 1024
                    h = (gcol % 1024) // 512
                    giB = MM(gh_ps[g][0:64, h * 512:(h + 1) * 512],
                             siT_bf[:, c * 64:(c + 1) * 64],
                             wihb[:, c * 1536 + col: c * 1536 + col + 512],
                             start=(c == 0), stop=False, waits=[(s_wih1, 128)])
            gi_done = None
            for g in range(3):
                for h in range(2):
                    gi_done = MM(gh_ps[g][0:64, h * 512:(h + 1) * 512],
                                 ones_r[0:1, 0:64],
                                 rows[0:1, g * 1024 + h * 512: g * 1024 + h * 512 + 512],
                                 start=False, stop=True,
                                 waits=[(s_bih, 16), SML, (pe, giB)])

            for i in range(8):
                GD(wihb[:, i * 1024:(i + 1) * 1024],
                   wvs_d[i * 128:(i + 1) * 128, :], s_wvs, waits=[(pe, gi_done)])
            for i in range(8):
                GD(wihb[:, 8192 + i * 512: 8192 + (i + 1) * 512],
                   wvp_d[i * 128:(i + 1) * 128, 0:512], s_wvpA,
                   waits=[(pe, gi_done)])
            GD(rows[0:1, 0:1024], bvpr_d, s_bvp)

            if _PH == 67 and _KSUB < 4:
                raise _Cut()
            t_r0 = TT("vector", g_r[:], psA[0:64, 0:1024], gh_bf[:, 0:1024], ALU.add,
                      waits=[(pe, gi_done), (ve, gh_done_v)], inc=ve)
            t_rg = AV(g_r[:], g_r[:], ACT.Sigmoid, waits=[(ve, t_r0)])
            t_z0 = TT("vector", g_z[:], psB[0:64, 0:1024], gh_bf[:, 1024:2048], ALU.add,
                      inc=ve)
            t_zg = AV(g_z[:], g_z[:], ACT.Sigmoid, waits=[(ve, t_z0)])
            TT("vector", g_n[:], g_r[:], gh_bf[:, 2048:3072], ALU.mult,
               waits=[(ae, t_rg)])
            t_n1 = TT("vector", g_n[:], psC[0:64, 0:1024], g_n[:], ALU.add, inc=ve)
            t_ng = AV(g_n[:], g_n[:], ACT.Tanh, waits=[(ve, t_n1)])
            TT("vector", sin_bf[0:64, :], sli_bf[:], g_n[:], ALU.subtract,
               waits=[(ae, t_ng), (ae, t_zg), SML])
            TT("vector", sin_bf[0:64, :], g_z[:], sin_bf[0:64, :], ALU.mult)
            t_s2 = TT("vector", snew_f[:], sin_bf[0:64, :], g_n[:], ALU.add, inc=ve)
            t_sbf = CP("vector", snew_bf[0:64, :], snew_f[:], inc=ve)
            snT_trn, snT_ev = transpose_64x1024(snew_bf, snewT, (ve, t_sbf))

            if _PH == 67 and _KSUB < 5:
                raise _Cut()
            qk_ev = {}
            qk_prev = None
            for which in range(2):
                wbase = 4352 if which == 0 else 0
                brow = 3136 if which == 0 else 3392
                wtok = (s_wqs, 128) if which == 0 else (s_wks, 128)
                for e2 in range(2):
                    reg = psD[0:128, (which * 2 + e2) * 64:(which * 2 + e2) * 64 + 64]
                    w = [(ve, snT_ev), wtok, SML]
                    if qk_prev is not None:
                        w.append((ve, qk_prev))
                    tq = None
                    for i in range(8):
                        tq = MM(reg, pool[:, wbase + i * 256 + e2 * 128:
                                          wbase + i * 256 + e2 * 128 + 128],
                                snewT[:, i * 64:(i + 1) * 64],
                                start=(i == 0), stop=False, waits=w)
                        w = ()
                    tq = MM(reg, rows[0:1, brow + e2 * 128: brow + e2 * 128 + 128],
                            ones_r[0:1, 0:64], start=False, stop=True)
                    dst = qT_bf if which == 0 else kT_bf
                    qk_ev[(which, e2)] = CP("vector", dst[:, e2 * 64:(e2 + 1) * 64], reg,
                                            waits=[(pe, tq)], inc=ve)
                    qk_prev = qk_ev[(which, e2)]
            tv = None
            if _PH == 67 and _KSUB < 6:
                raise _Cut()
            w = [(s_wvs, 128), (ve, snT_ev), (ve, t_n1), SML]
            for i in range(8):
                for h in range(2):
                    tv = MM(psC[0:64, h * 512:(h + 1) * 512],
                            snewT[:, i * 64:(i + 1) * 64],
                            wihb[:, i * 1024 + h * 512: i * 1024 + h * 512 + 512],
                            start=(i == 0), stop=False, waits=w)
                    w = ()
            for h in range(2):
                tv = MM(psC[0:64, h * 512:(h + 1) * 512], ones_r[0:1, 0:64],
                        rows[0:1, 3648 + h * 512: 3648 + h * 512 + 512],
                        start=False, stop=True)
            v_ev = CP("vector", v_bf[:], psC[0:64, 0:1024], waits=[(pe, tv)], inc=ve)

            for i in range(8):
                GD(wihb[:, i * 512:(i + 1) * 512],
                   wvp_d[i * 128:(i + 1) * 128, 512:1024], s_wvpB,
                   waits=[(pe, tv)])

            if _PH == 67 and _KSUB < 7:
                raise _Cut()
            tl = MM(psD[0:64, 256:320], qT_bf[:, 0:64], kT_bf[:, 0:64], True, False,
                    waits=[(ve, qk_ev[(0, 0)]), (ve, qk_ev[(0, 1)]),
                           (ve, qk_ev[(1, 0)]), (ve, qk_ev[(1, 1)])])
            tl = MM(psD[0:64, 256:320], qT_bf[:, 64:128], kT_bf[:, 64:128], False, True)
            RED(zrow[:, 0:1], psD[0:64, 256:320], ALU.max, waits=[(pe, tl)])
            t_mxs = TS("vector", zrow[:, 1:2], zrow[:, 0:1], -1.0 / 16.0, None, ALU.mult,
                       inc=vsm)
            t_E = kb.op("scalar",
                        lambda e: e.activation(out=E_bf[0:64, :],
                                               in_=psD[0:64, 256:320],
                                               func=ACT.Exp, bias=zrow[0:64, 1:2],
                                               scale=cst[0:64, 2:3]),
                        waits=[(vsm, t_mxs)], inc=ae,
                        r=[psD[0:64, 256:320], zrow[0:64, 1:2]],
                        w=[E_bf[0:64, :]])
            tE = TRN(psT[0:64, 384:512], E_bf[0:128, :], waits=[(ae, t_E)])
            ET_ev = CPT(ET_bf[:], psT[0:64, 384:448], waits=[(pe, tE)], inc=ve)
            tZ = MM(psD[0:64, 320:321], ET_bf[:], ones_c[0:64, :], True, True,
                    waits=[(ve, ET_ev)])
            tS = None
            w = [(ve, v_ev), (ve, t_r0)]
            for h in range(2):
                tS = MM(psA[0:64, h * 512:(h + 1) * 512], ET_bf[:],
                        v_bf[:, h * 512:(h + 1) * 512], True, True, waits=w)
                w = ()
            CP("vector", zrow[:, 2:3], psD[0:64, 320:321], waits=[(pe, tZ)])
            t_Zr = RCP(zrow[:, 3:4], zrow[:, 2:3], inc=vsm)
            t_ys = STT("vector", si_sb[0:64, 0:1024], psA[0:64, 0:1024], zrow[0:64, 3:4],
                       snew_f[:], ALU.mult, ALU.add,
                       waits=[(vsm, t_Zr), (pe, tS), (ve, t_s2)], inc=ve)
            BNS(bnt[0:64, 0:6], si_sb[0:64, 0:512], waits=[(ve, t_ys)])
            BNS(bnt[0:64, 6:12], si_sb[0:64, 512:1024])
            t_bns = BNA(stats[0:64, 0:2], bnt[0:64, 0:12], inc=ve)
            t_sqs = AV(stats[0:64, 2:3], stats[0:64, 1:2], ACT.Sqrt, bias=cst[0:64, 0:1], scale=1.0,
                       waits=[(ve, t_bns)])
            t_rcs = RCP(stats[0:64, 3:4], stats[0:64, 2:3], waits=[(ae, t_sqs)], inc=vsm)
            t_yn = TS("vector", snew_bf[0:64, :], si_sb[0:64, 0:1024],
                      stats[0:64, 0:1],
                      stats[0:64, 3:4], ALU.subtract, ALU.mult,
                      waits=[(vsm, t_rcs), (pe, snT_trn)], inc=ve)

            if _PH == 67 and _KSUB < 8:
                raise _Cut()
            tb2 = None
            w = [(s_wvpA, 128), (s_wvpB, 128), (s_bvp, 16), (ve, t_z0), SML]
            for i in range(8):
                tb2 = MM(psB[0:1, 0:512], lnb_bf[:, i:i + 1],
                         wihb[:, 8192 + i * 512: 8192 + (i + 1) * 512],
                         start=(i == 0), stop=False, waits=w)
                w = ()
                tb2 = MM(psB[0:1, 512:1024], lnb_bf[:, i:i + 1],
                         wihb[:, i * 512:(i + 1) * 512],
                         start=(i == 0), stop=False)
            tb2 = MM(psB[0:1, 0:512], ones_r[0:1, 0:1], rows[0:1, 0:512],
                     start=False, stop=True)
            tb2 = MM(psB[0:1, 512:1024], ones_r[0:1, 0:1], rows[0:1, 512:1024],
                     start=False, stop=True)
            bvp_ev = CP("vector", bvppr[:], psB[0:1, 0:1024], waits=[(pe, tb2)], inc=ve)
            if _PH == 67 and _KSUB < 9:
                raise _Cut()
            t_wvs = None
            for i in range(8):
                TS("vector", wihb[:, 8192 + i * 512: 8192 + (i + 1) * 512],
                   wihb[:, 8192 + i * 512: 8192 + (i + 1) * 512],
                   lng_sb[:, i:i + 1], None, ALU.mult, waits=[(pe, tb2), SML])
                t_wvs = TS("vector", wihb[:, i * 512:(i + 1) * 512],
                           wihb[:, i * 512:(i + 1) * 512],
                           lng_sb[:, i:i + 1], None, ALU.mult, inc=ve)
            if _PH == 67 and _KSUB < 10:
                raise _Cut()
            ynT_trn, ynT_ev = transpose_64x1024(snew_bf, snewT, (ve, t_yn))
            if _PH == 67 and _KSUB < 11:
                raise _Cut()
            tsv = None
            w = [(ve, ynT_ev), (ve, t_wvs), (ve, t_ys)]
            for i in range(8):
                tsv = MM(psA[0:64, 0:512], snewT[:, i * 64:(i + 1) * 64],
                         wihb[:, 8192 + i * 512: 8192 + (i + 1) * 512],
                         start=(i == 0), stop=False, waits=w)
                w = ()
                tsv = MM(psA[0:64, 512:1024], snewT[:, i * 64:(i + 1) * 64],
                         wihb[:, i * 512:(i + 1) * 512],
                         start=(i == 0), stop=False)
            tsv = MM(psA[0:64, 0:512], ones_r[0:1, 0:64], bvppr[0:1, 0:512],
                     start=False, stop=True, waits=[(ve, bvp_ev)])
            tsv = MM(psA[0:64, 512:1024], ones_r[0:1, 0:64], bvppr[0:1, 512:1024],
                     start=False, stop=True)
            sv_ev = CP("vector", sin_bf[0:64, :], psA[0:64, 0:1024],
                       waits=[(pe, tsv)], inc=ve)
            if _PH == 67 and _KSUB < 12:
                raise _Cut()
            svT_trn, svT_ev = transpose_64x1024(sin_bf, snewT, (ve, sv_ev))
            if _PH == 67 and _KSUB < 13:
                raise _Cut()
            tsp = None
            w = [(ve, svT_ev), (s_wof, 128)]
            for i in range(8):
                for h in range(2):
                    tsp = MM(psA[0:64, h * 512:(h + 1) * 512],
                             snewT[:, i * 64:(i + 1) * 64],
                             pool[:, 8192 + i * 1024 + h * 512:
                                  8192 + i * 1024 + h * 512 + 512],
                             start=(i == 0), stop=(i == 7), waits=w)
                    w = ()
            svp_ev = CP("vector", svpp[0:64, :], psA[0:64, 0:1024],
                        waits=[(pe, tsp)], inc=ve)

            if _PH < 70:
                raise _Cut()
            mix_ps = [psA, psB]
            v_done = [None] * NT
            stok = [None] * NT
            _KROWS = int(os.environ.get("KROWS", "65"))
            _KTILES = int(os.environ.get("KTILES", str(NT)))
            for t in range(NT):
                ps = mix_ps[t % 2]
                w = [(ve, svp_ev), (ve, alT_ev[NT - 1]), (ve, ve_const), SML]
                if t >= 2:
                    w.append((ve, v_done[t - 2]))
                tm = None
                if t >= _KTILES:
                    v_done[t] = ve.n
                    stok[t] = 0
                    continue
                for h in range(2):
                    tm = MM(ps[0:128, h * 512:(h + 1) * 512],
                            alphaT[0:_KROWS, t * 128:(t + 1) * 128],
                            svpp[0:_KROWS, h * 512:(h + 1) * 512],
                            start=True, stop=True, waits=w)
                    w = ()
                vs_ = fpA[:, (t % 2) * 1024:(t % 2) * 1024 + 1024]
                w = [(pe, tm)]
                if t >= 2:
                    w.append((s_st[t % 2], 16 * (t // 2)))
                if _PH == 70 and _KSUB < 2:
                    v_done[t] = ve.n
                    stok[t] = 0
                    continue
                t_vadd = TT("vector", vs_, ps[0:128, 0:1024],
                            x1bf[:, t * 1024:(t + 1) * 1024],
                            ALU.add, waits=w, inc=ve)
                if _PH == 70 and _KSUB < 3:
                    v_done[t] = t_vadd
                    stok[t] = 0
                    continue
                BNS(bnt[:, 0:6], vs_[:, 0:512])
                BNS(bnt[:, 6:12], vs_[:, 512:1024])
                t_bn = BNA(stats[:, 0:2], bnt[:, 0:12], inc=ve)
                t_sq = AV(stats[:, 2:3], stats[:, 1:2], ACT.Sqrt, bias=cst[:, 0:1], scale=1.0,
                          waits=[(ve, t_bn)])
                t_rc = RCP(stats[:, 3:4], stats[:, 2:3], waits=[(ae, t_sq)], inc=vsm)
                if _PH == 70 and _KSUB < 4:
                    v_done[t] = t_vadd
                    stok[t] = 0
                    continue
                if _PH == 70 and _KSUB == 6:
                    v_done[t] = TS("vector", topk[:, 0:64], vs_[:, 0:64],
                                   stats[:, 0:1], stats[:, 3:4],
                                   ALU.subtract, ALU.mult, waits=[(vsm, t_rc)],
                                   inc=ve)
                else:
                    v_done[t] = TS("vector", vs_, vs_, stats[:, 0:1], stats[:, 3:4],
                                   ALU.subtract, ALU.mult, waits=[(vsm, t_rc)], inc=ve)
                if _PH == 70 and _KSUB < 5:
                    stok[t] = 0
                    continue
                stok[t] = kb.op("sync",
                                lambda e, t=t, src=vs_: e.dma_start(
                                    out=out_hbm[t * 128:(t + 1) * 128, :], in_=src),
                                waits=[(ve, v_done[t])], inc=s_st[t % 2], inck=16)
            if not (_PH == 70 and _KSUB < 5):
                kb.op("sync", lambda e: None,
                      waits=[(s_st[0], 16 * 8), (s_st[1], 16 * 8)])
            else:
                kb.op("sync", lambda e: e.dma_start(
                    out=out_hbm[0:128, :], in_=fpA[:, 0:1024]),
                      waits=[(ve, ve.n), (pe, pe.n)], inc=s_st[0], inck=16)
                kb.op("sync", lambda e: None, waits=[(s_st[0], 16)])

            _emit_blocks(kb, nc)
        except _Cut:
            kb.op("sync", lambda e: None,
                  waits=[(pe, pe.n), (ve, ve.n), (ae, ae.n),
                         (vsm, vsm.n), (cc, cc.n)])
            kb.op("sync", lambda e: e.dma_start(
                out=out_hbm[0:128, :], in_=fpA[:, 0:1024]),
                  inc=s_st[0], inck=16)
            kb.op("sync", lambda e: None, waits=[(s_st[0], 16)])
            _emit_blocks(kb, nc)
    return nc


def _emit_blocks(kb, nc):
    with nc.Block() as block:
        block.tensor(kb.runner("tensor"))
        block.vector(kb.runner("vector"))
        block.scalar(kb.runner("scalar"))
        block.gpsimd(kb.runner("gpsimd"))
        block.sync(kb.runner("sync"))





_NC_CACHE = None


def kernel(L0_attn_x, L0_attn_W_q, L0_attn_W_k, L0_attn_W_v, L0_attn_W_o,
           L0_ffn_slot_init, L0_ffn_w1, L0_ffn_b1, L0_ffn_w2, L0_ffn_b2,
           L0_ffn_tau, L0_ffn_Wih, L0_ffn_Whh, L0_ffn_bih, L0_ffn_bhh,
           L0_ffn_wq, L0_ffn_bq, L0_ffn_wk, L0_ffn_bk, L0_ffn_wv, L0_ffn_bv,
           L0_ffn_ln_g, L0_ffn_ln_b, L0_ffn_wvp, L0_ffn_bvp, L0_ffn_wo,
           L0_ffn_bo):
    global _NC_CACHE
    f32 = np.float32
    asc = np.ascontiguousarray

    x = np.asarray(L0_attn_x, dtype=f32)
    B, S, _ = x.shape

    def chunkvec(v, n):
        return asc(np.asarray(v, f32).reshape(n, 128).T)

    common = {
        "wkT": asc(np.asarray(L0_attn_W_k, f32).T),
        "wvT": asc(np.asarray(L0_attn_W_v, f32).T),
        "wq": asc(np.asarray(L0_attn_W_q, f32)),
        "woT": asc(np.asarray(L0_attn_W_o, f32).T),
        "w1a": asc(np.asarray(L0_ffn_w1, f32)[:D]),
        "w1b": asc(np.asarray(L0_ffn_w1, f32)[D:]),
        "w2": asc(np.asarray(L0_ffn_w2, f32)),
        "b1c": chunkvec(L0_ffn_b1, 4),
        "b2r": asc(np.asarray(L0_ffn_b2, f32).reshape(1, 64)),
        "tau": asc(np.asarray(L0_ffn_tau, f32).reshape(1, 1)),
        "wihT": asc(np.asarray(L0_ffn_Wih, f32).T),
        "whhT": asc(np.asarray(L0_ffn_Whh, f32).T),
        "bihr": asc(np.asarray(L0_ffn_bih, f32).reshape(1, 3072)),
        "bhhr": asc(np.asarray(L0_ffn_bhh, f32).reshape(1, 3072)),
        "sli": asc(np.asarray(L0_ffn_slot_init, f32)),
        "sliTp": asc(np.asarray(L0_ffn_slot_init, f32).T
                     .reshape(8, 128, 64).transpose(1, 0, 2).reshape(128, 512)),
        "wqs": asc(np.asarray(L0_ffn_wq, f32)),
        "bqr": asc(np.asarray(L0_ffn_bq, f32).reshape(1, 256)),
        "wks": asc(np.asarray(L0_ffn_wk, f32)),
        "bkr": asc(np.asarray(L0_ffn_bk, f32).reshape(1, 256)),
        "wvs": asc(np.asarray(L0_ffn_wv, f32)),
        "bvr": asc(np.asarray(L0_ffn_bv, f32).reshape(1, D)),
        "lngp": chunkvec(L0_ffn_ln_g, 8),
        "lnbp": chunkvec(L0_ffn_ln_b, 8),
        "wvp": asc(np.asarray(L0_ffn_wvp, f32)),
        "bvpr": asc(np.asarray(L0_ffn_bvp, f32).reshape(1, D)),
        "wof": asc(np.asarray(L0_ffn_wo, f32)),
        "bor": asc(np.asarray(L0_ffn_bo, f32).reshape(1, D)),
        "idn": np.eye(128, dtype=f32),
    }

    in_maps = []
    for core in range(8):
        b = core // 4
        s0 = (core % 4) * T
        shard = asc(x[b, s0:s0 + T, :])
        m = dict(common)
        m["x_tm"] = shard
        m["xT"] = asc(shard.T)
        in_maps.append(m)

    if _NC_CACHE is None:
        _NC_CACHE = build_program()
    res = run_bass_kernel_spmd(_NC_CACHE, in_maps, core_ids=list(range(8)))
    out = np.empty((B, S, D), dtype=f32)
    for core in range(8):
        b = core // 4
        s0 = (core % 4) * T
        out[b, s0:s0 + T, :] = res.results[core]["out"]
    return out


# version 08 (final)
# speedup vs baseline: 1.5394x; 1.5394x over previous
import os
import numpy as np
from contextlib import ExitStack

_PH = int(os.environ.get("KPH", "99"))
_KSUB = int(os.environ.get("KSUB", "9"))


class _Cut(Exception):
    pass


import concourse.bass as bass
import concourse.mybir as mybir
from concourse.bass_utils import run_bass_kernel_spmd

F32 = mybir.dt.float32
BF16 = mybir.dt.bfloat16
AX = mybir.AxisListType
ALU = mybir.AluOpType
ACT = mybir.ActivationFunctionType

T = 2048
NT = 16
D = 1024


class Sem:
    def __init__(self, ctx, nc, name):
        self.h = ctx.enter_context(nc.semaphore(name))
        self.n = 0

    def bump(self, k):
        self.n += k
        return self.n


def _rng(ap):
    try:
        name = ap.tensor.name
        lo = ap.offset
        hi = lo + 1
        for step, count in ap.ap:
            hi += abs(step) * (count - 1)
        return (name, lo, hi)
    except Exception:
        return None


def _overlap(aset, bset):
    for a in aset:
        if a is None:
            continue
        for b in bset:
            if b is None:
                continue
            if a[0] == b[0] and a[1] < b[2] and b[1] < a[2]:
                return True
    return False


class KB:
    ENGINES = ("tensor", "vector", "scalar", "gpsimd", "sync")

    def __init__(self, selfsems):
        self.prog = {e: [] for e in self.ENGINES}
        self.guaranteed = {e: {} for e in self.ENGINES}
        self.last = {e: None for e in self.ENGINES}
        self.selfsems = selfsems

    def op(self, eng, f, waits=(), inc=None, inck=1, r=(), w=()):
        reads = [_rng(a) for a in r]
        writes = [_rng(a) for a in w]
        waits = list(waits)
        prev = self.last[eng]
        if prev is not None:
            p_reads, p_writes, p_rec = prev
            if _overlap(p_writes, reads) or _overlap(p_writes, writes)                     or _overlap(p_reads, writes):
                if p_rec[2] is None:
                    ss = self.selfsems[eng]
                    tok = ss.bump(1)
                    p_rec[2] = ss.h
                    p_rec[3] = 1
                    waits.append((ss, tok))
                else:
                    waits.append((p_rec[4], p_rec[5]))
        g = self.guaranteed[eng]
        wl = []
        for s, v in waits:
            if v is None:
                continue
            if v > 0 and g.get(s, -1) < v:
                wl.append((s.h, v))
                g[s] = v
        tok = inc.bump(inck) if inc is not None else None
        rec = [wl, f, inc.h if inc is not None else None, inck,
               inc, tok]
        self.prog[eng].append(rec)
        self.last[eng] = (reads, writes, rec)
        return tok

    def runner(self, eng):
        entries = self.prog[eng]
        def run(e):
            for wl, f, inch, inck, _s, _t in entries:
                for h, v in wl:
                    e.wait_ge(h, v)
                inst = f(e)
                if inch is not None:
                    inst.then_inc(inch, inck)
        return run


def build_program():
    nc = bass.Bass("TRN2", target_bir_lowering=False)

    def din(name, shape):
        return nc.dram_tensor(name, list(shape), F32, kind="ExternalInput").ap()

    x_tm = din("x_tm", (T, D))
    xT_d = din("xT", (D, T))
    wkT_d = din("wkT", (D, D)); wvT_d = din("wvT", (D, D))
    wq_d = din("wq", (D, D)); woT_d = din("woT", (D, D))
    w1a_d = din("w1a", (D, 512)); w1b_d = din("w1b", (D, 512))
    w2_d = din("w2", (512, 64))
    b1c_d = din("b1c", (128, 4)); b2r_d = din("b2r", (1, 64))
    tau_d = din("tau", (1, 1))
    wihT_d = din("wihT", (D, 3072)); whhT_d = din("whhT", (D, 3072))
    bihr_d = din("bihr", (1, 3072)); bhhr_d = din("bhhr", (1, 3072))
    sli_d = din("sli", (64, D)); sliTp_d = din("sliTp", (128, 512))
    wqs_d = din("wqs", (D, 256)); bqr_d = din("bqr", (1, 256))
    wks_d = din("wks", (D, 256)); bkr_d = din("bkr", (1, 256))
    wvs_d = din("wvs", (D, D)); bvr_d = din("bvr", (1, D))
    lngp_d = din("lngp", (128, 8)); lnbp_d = din("lnbp", (128, 8))
    wvp_d = din("wvp", (D, D)); bvpr_d = din("bvpr", (1, D))
    wof_d = din("wof", (D, D)); bor_d = din("bor", (1, D))
    idn_d = din("idn", (128, 128))
    out_hbm = nc.dram_tensor("out", [T, D], F32, kind="ExternalOutput").ap()

    vk_bi = nc.dram_tensor("vk_bi", [128, 1024], F32).ap()
    vk_bo = nc.dram_tensor("vk_bo", [128, 1024], F32).ap()
    si_bi = nc.dram_tensor("si_bi", [64, 1040], F32).ap()
    si_bo = nc.dram_tensor("si_bo", [64, 1040], F32).ap()
    RG = [[0, 1, 2, 3], [4, 5, 6, 7]]

    with ExitStack() as ctx:
        E = ctx.enter_context
        xTc = [E(nc.sbuf_tensor(f"xTc{i}", [128, T], BF16)) for i in range(8)]
        x1bf = E(nc.sbuf_tensor("x1bf", [128, NT * 1024], BF16))
        kvr = E(nc.sbuf_tensor("kvr", [128, 4096], BF16))
        pool = E(nc.sbuf_tensor("pool", [128, 16384], BF16))
        wihb = E(nc.sbuf_tensor("wihb", [128, 12288], BF16))
        gh_bf = E(nc.sbuf_tensor("gh_bf", [64, 3072], BF16))
        fpA = E(nc.sbuf_tensor("fpA", [128, 2048], F32))
        rows = E(nc.sbuf_tensor("rows", [1, 4672], BF16))
        alpha_bf = E(nc.sbuf_tensor("alpha_bf", [128, NT * 64], BF16))
        alphaT = E(nc.sbuf_tensor("alphaT", [128, T], BF16))
        topk = E(nc.sbuf_tensor("topk", [128, 448], F32))
        w2_sb = E(nc.sbuf_tensor("w2_sb", [128, 256], BF16))
        vk_bf = E(nc.sbuf_tensor("vk_bf", [128, 1024], BF16))
        id_bf = E(nc.sbuf_tensor("id_bf", [128, 128], BF16))
        ones_r = E(nc.sbuf_tensor("ones_r", [1, 128], BF16))
        ones64 = E(nc.sbuf_tensor("ones64", [64, 1], BF16))
        ones_c = E(nc.sbuf_tensor("ones_c", [128, 1], BF16))
        mT_f = E(nc.sbuf_tensor("mT_f", [128, 8], F32))
        c_sb = E(nc.sbuf_tensor("c_sb", [128, 4], F32))
        b1_sb = E(nc.sbuf_tensor("b1_sb", [128, 4], F32))
        lng_sb = E(nc.sbuf_tensor("lng_sb", [128, 8], F32))
        lnb_bf = E(nc.sbuf_tensor("lnb_bf", [128, 8], BF16))
        tau_sb = E(nc.sbuf_tensor("tau_sb", [1, 1], F32))
        a_sb = E(nc.sbuf_tensor("a_sb", [1, 1], F32))
        onesf = E(nc.sbuf_tensor("onesf", [1, 128], F32))
        a_b = E(nc.sbuf_tensor("a_b", [128, 1], F32))
        cst = E(nc.sbuf_tensor("cst", [128, 3], F32))
        stats = E(nc.sbuf_tensor("stats", [128, 8], F32))
        bnt = E(nc.sbuf_tensor("bnt", [128, 12], F32))
        sli_bf = E(nc.sbuf_tensor("sli_bf", [64, D], BF16))
        sliT = E(nc.sbuf_tensor("sliT", [128, 512], BF16))
        si_sb = E(nc.sbuf_tensor("si_sb", [64, 1040], F32))
        sin_bf = E(nc.sbuf_tensor("sin_bf", [128, D], BF16))
        siT_bf = E(nc.sbuf_tensor("siT_bf", [128, 512], BF16))
        g_r = E(nc.sbuf_tensor("g_r", [64, D], BF16))
        g_z = E(nc.sbuf_tensor("g_z", [64, D], BF16))
        g_n = E(nc.sbuf_tensor("g_n", [64, D], BF16))
        snew_f = E(nc.sbuf_tensor("snew_f", [64, D], F32))
        snew_bf = E(nc.sbuf_tensor("snew_bf", [128, D], BF16))
        snewT = E(nc.sbuf_tensor("snewT", [128, 512], BF16))
        qT_bf = E(nc.sbuf_tensor("qT_bf", [128, 128], BF16))
        kT_bf = E(nc.sbuf_tensor("kT_bf", [128, 128], BF16))
        v_bf = E(nc.sbuf_tensor("v_bf", [64, D], BF16))
        E_bf = E(nc.sbuf_tensor("E_bf", [128, 64], BF16))
        ET_bf = E(nc.sbuf_tensor("ET_bf", [64, 64], BF16))
        zrow = E(nc.sbuf_tensor("zrow", [64, 4], F32))
        svpp = E(nc.sbuf_tensor("svpp", [128, D], BF16))
        bvppr = E(nc.sbuf_tensor("bvppr", [1, D], BF16))

        psA = E(nc.psum_tensor("psA", [128, 1024], F32))
        psB = E(nc.psum_tensor("psB", [128, 1024], F32))
        psC = E(nc.psum_tensor("psC", [128, 1024], F32))
        psD = E(nc.psum_tensor("psD", [128, 512], F32))
        psT = E(nc.psum_tensor("psT", [128, 512], F32))

        pe = Sem(ctx, nc, "pe")
        ve = Sem(ctx, nc, "ve")
        ae = Sem(ctx, nc, "ae")
        cc = Sem(ctx, nc, "cc")
        vsm = Sem(ctx, nc, "vsm")
        _sems = {}
        def SEMG(name):
            if name not in _sems:
                _sems[name] = Sem(ctx, nc, name)
            return _sems[name]
        s_sml = SEMG("s_sml"); s_wk = SEMG("s_wk")
        s_x0a = SEMG("s_x0a"); s_x0b = SEMG("s_x0b"); s_x0c = SEMG("s_x0c")
        s_wv = SEMG("s_wv"); s_whh = SEMG("s_whh"); s_woT = SEMG("s_woT")
        s_wq = SEMG("s_wq"); s_bih = SEMG("s_bih"); s_wof = SEMG("s_wof")
        s_w1a = SEMG("s_w1a"); s_wih0 = SEMG("s_wih0"); s_w2 = SEMG("s_w2")
        s_wqs = SEMG("s_wqs"); s_wks = SEMG("s_wks"); s_wih1 = SEMG("s_wih1")
        s_wvs = SEMG("s_wvs"); s_wvpA = SEMG("s_wvpA"); s_bvp = SEMG("s_bvp")
        s_wvpB = SEMG("s_wvpB")
        s_a1o = SEMG("s_a1o"); s_a1i = SEMG("s_a1i")
        s_a2o = SEMG("s_a2o"); s_a2i = SEMG("s_a2i")
        s_x = [SEMG("s_xa"), SEMG("s_xb")]
        s_st = [SEMG("s_sta"), SEMG("s_stb")]
        selfsems = {e: SEMG("s_self_" + e) for e in KB.ENGINES}

        kb = KB(selfsems)

        def MM(out, lhsT, rhs, start, stop, waits=(), fin=None):
            inc_ = pe if (stop if fin is None else fin) else None
            return kb.op("tensor",
                         lambda e, o=out, l=lhsT, r=rhs, s=start, p=stop:
                         e.matmul(o, l, r, start=s, stop=p), waits=waits,
                         inc=inc_)

        _last_psT = [None]

        def TRN(out, in_, waits=()):
            p = in_.shape[0]
            w = list(waits) + [(s_sml, 208)]
            if _last_psT[0] is not None:
                w.append((ve, _last_psT[0]))
            return kb.op("tensor",
                         lambda e, o=out, i=in_, d=id_bf[0:p, 0:p]:
                         e.matmul(o, i, d, start=True, stop=True),
                         waits=w, inc=pe,
                         r=[in_], w=[out])

        def CPT(out, in_, waits=(), inc=None):
            tok = kb.op("vector", lambda e, o=out, i=in_: e.tensor_copy(o, i),
                        waits=waits, inc=inc, r=[in_], w=[out])
            _last_psT[0] = tok
            return tok

        def CP(eng, out, in_, waits=(), inc=None):
            return kb.op(eng, lambda e, o=out, i=in_: e.tensor_copy(o, i),
                         waits=waits, inc=inc, r=[in_], w=[out])

        def TT(eng, out, a, b, aop, waits=(), inc=None):
            return kb.op(eng, lambda e, o=out, x=a, y=b, p=aop:
                         e.tensor_tensor(o, x, y, p), waits=waits, inc=inc,
                         r=[a, b], w=[out])

        def TS(eng, out, in0, s1, s2, op0, op1=None, waits=(), inc=None):
            rr = [in0] + [x for x in (s1, s2) if hasattr(x, "tensor")]
            if op1 is None:
                return kb.op(eng, lambda e, o=out, i=in0, a=s1, p=op0:
                             e.tensor_scalar(o, i, a, None, p), waits=waits,
                             inc=inc, r=rr, w=[out])
            return kb.op(eng, lambda e, o=out, i=in0, a=s1, b=s2, p=op0, q=op1:
                         e.tensor_scalar(o, i, a, b, p, q), waits=waits,
                         inc=inc, r=rr, w=[out])

        def STT(eng, out, in0, sc, in1, op0, op1, waits=(), inc=None):
            rr = [in0, in1] + [x for x in (sc,) if hasattr(x, "tensor")]
            return kb.op(eng, lambda e, o=out, i=in0, s=sc, j=in1, p=op0, q=op1:
                         e.scalar_tensor_tensor(o, i, s, j, p, q), waits=waits,
                         inc=inc, r=rr, w=[out])

        def AV(out, in_, func, bias=0.0, scale=1.0, waits=()):
            rr = [in_] + [x for x in (bias, scale) if hasattr(x, "tensor")]
            return kb.op("scalar",
                         lambda e, o=out, i=in_, f=func, b=bias, s=scale:
                         e.activation(out=o, in_=i, func=f, bias=b, scale=s),
                         waits=waits, inc=ae, r=rr, w=[out])

        def RED(out, in_, aop, waits=(), inc=None):
            return kb.op("vector", lambda e, o=out, i=in_, p=aop:
                         e.tensor_reduce(o, i, AX.X, p), waits=waits, inc=inc,
                         r=[in_], w=[out])

        def RCP(out, in_, waits=(), inc=None):
            return kb.op("vector", lambda e, o=out, i=in_: e.reciprocal(o, i),
                         waits=waits, inc=inc, r=[in_], w=[out])

        def MSET(eng, ap, val, waits=(), inc=None):
            return kb.op(eng, lambda e, a=ap, v=val: e.memset(a, v),
                         waits=waits, inc=inc, w=[ap])

        def GD(dst, src, sem, waits=()):
            return kb.op("gpsimd", lambda e, d=dst, s=src: e.dma_start(out=d, in_=s),
                         waits=waits, inc=sem, inck=16)

        def SD(dst, src, sem, waits=()):
            return kb.op("scalar", lambda e, d=dst, s=src: e.dma_start(out=d, in_=s),
                         waits=waits, inc=sem, inck=16)

        def BNS(out, in_, waits=(), inc=None):
            return kb.op("vector", lambda e, o=out, i=in_: e.bn_stats(o, i),
                         waits=waits, inc=inc, r=[in_], w=[out])

        def BNA(out, in_, waits=(), inc=None):
            return kb.op("vector", lambda e, o=out, i=in_: e.bn_aggr(o, i),
                         waits=waits, inc=inc, r=[in_], w=[out])

        try:
            for c in range(8):
                GD(xTc[c][:, 0:256], xT_d[c * 128:(c + 1) * 128, 0:256], s_x0a)
            for i in range(8):
                GD(pool[:, i * 1024:(i + 1) * 1024],
                   wkT_d[i * 128:(i + 1) * 128, :], s_wk)
            for i in range(8):
                GD(pool[:, 8192 + i * 1024: 8192 + (i + 1) * 1024],
                   wvT_d[i * 128:(i + 1) * 128, :], s_wv)
            for c in range(8):
                GD(xTc[c][:, 256:1024], xT_d[c * 128:(c + 1) * 128, 256:1024],
                   s_x0b)
            for c in range(8):
                GD(xTc[c][:, 1024:2048], xT_d[c * 128:(c + 1) * 128, 1024:2048],
                   s_x0c)
            for c in range(5):
                GD(x1bf[:, c * 3072:(c + 1) * 3072],
                   whhT_d[c * 128:(c + 1) * 128, :], s_whh)
            for c in range(5, 8):
                GD(wihb[:, (c - 5) * 3072:(c - 4) * 3072],
                   whhT_d[c * 128:(c + 1) * 128, :], s_whh)
            WHH = (s_whh, 128)
            WK = (s_wk, 128)
            WV = (s_wv, 128)
            X0A = (s_x0a, 128)
            X0B = (s_x0b, 128)
            X0C = (s_x0c, 128)
            GD(id_bf[:], idn_d, s_sml)
            GD(sli_bf[:], sli_d, s_sml)
            GD(sliT[:], sliTp_d, s_sml)
            GD(tau_sb[:], tau_d, s_sml)
            GD(rows[0:1, 0:3072], bhhr_d, s_sml)
            GD(rows[0:1, 3072:3136], b2r_d, s_sml)
            GD(rows[0:1, 3136:3392], bqr_d, s_sml)
            GD(rows[0:1, 3392:3648], bkr_d, s_sml)
            GD(rows[0:1, 3648:4672], bvr_d, s_sml)
            GD(lng_sb[:], lngp_d, s_sml)
            GD(lnb_bf[:], lnbp_d, s_sml)
            GD(b1_sb[:], b1c_d, s_sml)
            GD(svpp[64:65, :], bor_d, s_sml)
            SML = (s_sml, 208)
            for f in range(4):
                GD(w2_sb[:, f * 64:(f + 1) * 64],
                   w2_d[f * 128:(f + 1) * 128, :], s_w2)
            s_w1b = [SEMG("s_w1b%d" % w) for w in range(4)]
            for i in range(2):
                SD(fpA[:, 1024 + i * 512:1024 + (i + 1) * 512],
                   w1b_d[i * 128:(i + 1) * 128, :], s_w1b[0])

            MSET("vector", ones_r[:], 1.0)
            MSET("vector", ones64[:], 1.0 / 64.0)
            MSET("vector", ones_c[:], 1.0)
            MSET("vector", onesf[:], 1.0)
            MSET("vector", cst[:, 0:1], 1e-5)
            MSET("vector", si_sb[0:64, 1024:1040], 0.0)
            MSET("vector", sin_bf[64:128, :], 0.0)
            MSET("vector", snew_bf[64:128, :], 0.0)
            MSET("vector", E_bf[64:128, :], 0.0)
            MSET("vector", cst[:, 1:2], 0.125)
            MSET("vector", cst[:, 2:3], 1.0 / 16.0)
            ve_const = MSET("vector", alphaT[64:65, :], 1.0, inc=ve)

            t_abs = AV(a_sb[:], tau_sb[:], ACT.Abs, waits=[SML])
            t_a1 = TS("vector", a_sb[:], a_sb[:], 0.1, None, ALU.add,
                      waits=[(ae, t_abs)], inc=vsm)
            t_a2 = RCP(a_sb[:], a_sb[:], waits=[(vsm, t_a1)], inc=ve)

            kh = [[None, None] for _ in range(NT)]
            vh = [[None, None] for _ in range(NT)]
            k_mm_done = [None] * NT
            v_mm_done = [None] * NT
            vk_add = [None] * NT
            vk_done = None
            t_ab = t_mTe = t_c = t_cpre = None
            c_last = None

            def emit_vk(tp):
                nonlocal vk_done
                w = [(ve, kh[tp][1]), (ve, vh[tp][1])]
                if tp >= 1:
                    w.append((ve, vk_add[tp - 1]))
                for p in range(8):
                    vk_done = MM(psC[:, p * 128:(p + 1) * 128],
                                 kvr[:, 2048 + (tp % 2) * 1024 + p * 128:
                                      2048 + (tp % 2) * 1024 + (p + 1) * 128],
                                 kvr[:, (tp % 2) * 1024 + p * 128:
                                      (tp % 2) * 1024 + (p + 1) * 128],
                                 start=True, stop=True, waits=w)
                    w = ()
                if tp == 0:
                    vk_add[0] = CP("vector", fpA[:, 0:1024], psC[:, 0:1024],
                                   waits=[(pe, vk_done)], inc=ve)
                else:
                    vk_add[tp] = TT("vector", fpA[:, 0:1024], psC[:, 0:1024],
                                    fpA[:, 0:1024], ALU.add,
                                    waits=[(pe, vk_done)], inc=ve)

            for t in range(NT):
                xg = X0A if t < 2 else (X0B if t < 8 else X0C)
                tk = None
                for h in range(2):
                    w = [xg, WK]
                    if t >= 1:
                        w.append((ve, kh[t - 1][h]))
                    for i in range(8):
                        tk = MM(psB[:, h * 512:(h + 1) * 512],
                                xTc[i][:, t * 128:(t + 1) * 128],
                                pool[:, i * 1024 + h * 512:
                                     i * 1024 + h * 512 + 512],
                                start=(i == 0), stop=(i == 7), waits=w)
                        w = ()
                    kh[t][h] = CP("vector",
                                  kvr[:, (t % 2) * 1024 + h * 512:
                                       (t % 2) * 1024 + (h + 1) * 512],
                                  psB[:, h * 512:(h + 1) * 512],
                                  waits=[(pe, tk)], inc=ve)
                k_mm_done[t] = tk
                tv = None
                for h in range(2):
                    w = [xg, WV]
                    if t >= 1:
                        w.append((ve, vh[t - 1][h]))
                    for i in range(8):
                        tv = MM(psA[:, h * 512:(h + 1) * 512],
                                xTc[i][:, t * 128:(t + 1) * 128],
                                pool[:, 8192 + i * 1024 + h * 512:
                                     8192 + i * 1024 + h * 512 + 512],
                                start=(i == 0), stop=(i == 7), waits=w)
                        w = ()
                    vh[t][h] = CP("vector",
                                  kvr[:, 2048 + (t % 2) * 1024 + h * 512:
                                       2048 + (t % 2) * 1024 + (h + 1) * 512],
                                  psA[:, h * 512:(h + 1) * 512],
                                  waits=[(pe, tv)], inc=ve)
                v_mm_done[t] = tv
                if t >= 1:
                    emit_vk(t - 1)
                if t == 5:
                    t_abmm = MM(psD[0:128, 16:17], onesf[:], a_sb[:], True, True,
                                waits=[(ve, t_a2)])
                    t_ab = CP("vector", a_b[:], psD[0:128, 16:17],
                              waits=[(pe, t_abmm)], inc=ve)
                    t_mT = None
                    for c in range(8):
                        t_mT = MM(psD[0:128, c:c + 1],
                                  sli_bf[:, c * 128:(c + 1) * 128], ones64[:],
                                  True, True, waits=[SML])
                    t_mTe = CP("vector", mT_f[:], psD[0:128, 0:8],
                               waits=[(pe, t_mT)], inc=ve)
                if 6 <= t <= 9:
                    wv_ = t - 6
                    for f in range(4):
                        for i in range(2):
                            c_last = MM(psD[0:128, 8 + f:9 + f],
                                        fpA[:, 1024 + i * 512 + f * 128:
                                             1024 + i * 512 + (f + 1) * 128],
                                        mT_f[:, 2 * wv_ + i: 2 * wv_ + i + 1],
                                        start=(wv_ == 0 and i == 0),
                                        stop=(wv_ == 3 and i == 1),
                                        fin=(f == 3 and i == 1),
                                        waits=([(s_w1b[wv_], 32), (ve, t_mTe)]
                                               if (f == 0 and i == 0) else ()))
                    if wv_ < 3:
                        for i in range(2):
                            SD(fpA[:, 1024 + i * 512:1024 + (i + 1) * 512],
                               w1b_d[(2 * (wv_ + 1) + i) * 128:
                                     (2 * (wv_ + 1) + i + 1) * 128, :],
                               s_w1b[wv_ + 1],
                               waits=[(pe, c_last)] if i == 0 else ())
                if t == 10:
                    t_cpre = CP("vector", c_sb[:], psD[0:128, 8:12],
                                waits=[(pe, c_last)], inc=ve)
                    t_c = TT("vector", c_sb[:], c_sb[:], b1_sb[:], ALU.add,
                             waits=[SML], inc=ve)
            emit_vk(NT - 1)
            vk_ev = vk_add[NT - 1]

            vk_ev = vk_add[NT - 1]
            kb.op("gpsimd", lambda e: e.dma_start(out=vk_bi, in_=fpA[:, 0:1024]),
                  waits=[(ve, vk_ev)], inc=s_a1o, inck=16)
            t_cc1 = kb.op("gpsimd",
                          lambda e: e.collective_compute("AllReduce", ALU.add,
                                                         replica_groups=RG,
                                                         ins=[vk_bi.opt()],
                                                         outs=[vk_bo.opt()]),
                          waits=[(s_a1o, 16)], inc=cc)
            kb.op("gpsimd", lambda e: e.dma_start(out=vk_bf[:], in_=vk_bo),
                  waits=[(cc, t_cc1)], inc=s_a1i, inck=16)
            if _PH < 22:
                raise _Cut()
            t_z = None
            for p in range(8):
                MSET("vector", vk_bf[0:64, p * 128 + 64: p * 128 + 128], 0.0,
                     waits=[(s_a1i, 16)])
                t_z = MSET("vector", vk_bf[64:128, p * 128: p * 128 + 64], 0.0, inc=ve)

            if _PH < 25:
                raise _Cut()
            gh_ps = [psA, psB, psC]
            gh_done = None
            _NG = 3 if _PH >= 27 else (2 if _PH >= 26 else 1)
            for g in range(_NG):
                for c in range(8):
                    rhs_base = (x1bf if c < 5 else wihb)
                    off = c * 3072 if c < 5 else (c - 5) * 3072
                    for h in range(2):
                        w = []
                        if g == 0 and c == 0 and h == 0:
                            w = [WHH, SML,
                                 (ve, vh[NT - 1][1]), (ve, vk_ev)]
                        gh_done = MM(gh_ps[g][0:64, h * 512:(h + 1) * 512],
                                     sliT[:, c * 64:(c + 1) * 64],
                                     rhs_base[:, off + g * 1024 + h * 512:
                                              off + g * 1024 + h * 512 + 512],
                                     start=(c == 0), stop=False, waits=w)
                for h in range(2):
                    gh_done = MM(gh_ps[g][0:64, h * 512:(h + 1) * 512],
                                 ones_r[0:1, 0:64],
                                 rows[0:1, g * 1024 + h * 512: g * 1024 + h * 512 + 512],
                                 start=False, stop=True, waits=[SML])
            gh_ev = []
            for g in range(_NG):
                gh_ev.append(CP("vector", gh_bf[:, g * 1024:(g + 1) * 1024],
                                gh_ps[g][0:64, 0:1024], waits=[(pe, gh_done)], inc=ve))
            if _PH < 28:
                raise _Cut()
            gh_done_v = gh_ev[-1]

            if _PH < 30:
                raise _Cut()
            for j in range(8):
                GD(pool[:, j * 1024:(j + 1) * 1024],
                   woT_d[j * 128:(j + 1) * 128, :], s_woT,
                   waits=[(pe, k_mm_done[NT - 1])])
            for j in range(8):
                GD(wihb[:, j * 1024:(j + 1) * 1024],
                   wq_d[j * 128:(j + 1) * 128, :], s_wq, waits=[(pe, gh_done)])
            GD(rows[0:1, 0:3072], bihr_d, s_bih)

            m_evs = []
            for j in range(8):
                for h in range(2):
                    k2 = j * 2 + h
                    ps_m = psD if k2 % 2 == 0 else psT
                    w = []
                    if k2 == 0:
                        w = [(ve, t_z), (s_woT, 128), (pe, v_mm_done[NT - 1]),
                             (ve, t_cpre)]
                    if k2 >= 2:
                        w.append((ve, m_evs[k2 - 2]))
                    tmj = MM(ps_m[0:128, 0:512], vk_bf[:, j * 128:(j + 1) * 128],
                             pool[:, j * 1024 + h * 512: j * 1024 + h * 512 + 512],
                             start=True, stop=True, waits=w)
                    m_evs.append(CP("vector",
                                    pool[:, 8192 + j * 1024 + h * 512:
                                         8192 + j * 1024 + h * 512 + 512],
                                    ps_m[0:128, 0:512],
                                    waits=[(pe, tmj)], inc=ve))
            m_last_ev = m_evs[-1]
            n_ev = None
            n_last_mm = None
            n_ev_h = {}
            for i in range(8):
                for h in range(2):
                    for j in range(8):
                        w = []
                        if i == 0 and h == 0 and j == 0:
                            w = [(s_wq, 128), (ve, m_last_ev), (pe, gh_done),
                                 (ve, gh_ev[1])]
                        if j == 0 and i >= 1:
                            w = list(w) + [(ae, n_ev_h[h])]
                        n_last_mm = MM(psB[0:128, h * 512:(h + 1) * 512],
                                       wihb[:, j * 1024 + i * 128:
                                            j * 1024 + (i + 1) * 128],
                                       pool[:, 8192 + j * 1024 + h * 512:
                                            8192 + j * 1024 + h * 512 + 512],
                                       start=(j == 0), stop=(j == 7), waits=w)
                    n_ev = kb.op("scalar",
                                 lambda e, o=pool[:, i * 1024 + h * 512:
                                                  i * 1024 + h * 512 + 512],
                                 p=psB[0:128, h * 512:(h + 1) * 512]:
                                 e.activation(out=o, in_=p, func=ACT.Copy,
                                              bias=0.0, scale=cst[:, 1:2]),
                                 waits=[(pe, n_last_mm)], inc=ae,
                                 r=[psB[0:128, h * 512:(h + 1) * 512]],
                                 w=[pool[:, i * 1024 + h * 512:
                                         i * 1024 + h * 512 + 512]])
                    n_ev_h[h] = n_ev

            for i in range(8):
                GD(kvr[:, i * 512:(i + 1) * 512],
                   w1a_d[i * 128:(i + 1) * 128, :], s_w1a, waits=[(pe, vk_done)])
            for c in range(8):
                GD(wihb[:, c * 1536:(c + 1) * 1536],
                   wihT_d[c * 128:(c + 1) * 128, 0:1536], s_wih0,
                   waits=[(pe, n_last_mm)])

            x_tok = [None] * NT
            y_done = [None] * NT
            x1_done = [None] * NT
            att_done = [None] * NT
            att_h0 = [None] * NT
            tt_add = [None] * NT
            tp_ev = [None] * (NT * 8)
            alpha_done = [None] * NT
            alT_ev = [None] * NT
            lcp = [None] * NT
            lg_t = [None] * NT
            ee = {}
            h_ev = {}
            si_done = None
            lg_done = None

            def load_x(t):
                w = []
                if t == 0:
                    w.append((s_a1o, 16))
                if t >= 2:
                    w.append((ve, y_done[t - 2]))
                x_tok[t] = SD(fpA[:, (t % 2) * 1024:(t % 2) * 1024 + 1024],
                              x_tm[t * 128:(t + 1) * 128, :], s_x[t % 2], waits=w)

            load_x(0)
            load_x(1)

            def emit_transposes(t):
                for c in range(8):
                    w = []
                    if c == 0:
                        w.append((ve, x1_done[t]))
                    tt = TRN(psT[:, (c % 4) * 128:(c % 4) * 128 + 128],
                             x1bf[:, t * 1024 + c * 128: t * 1024 + (c + 1) * 128],
                             waits=w)
                    tp_ev[t * 8 + c] = CPT(xTc[c][:, t * 128:(t + 1) * 128],
                                           psT[:, (c % 4) * 128:(c % 4) * 128 + 128],
                                           waits=[(pe, tt)], inc=ve)

            def emit_h(tb):
                for f in range(4):
                    k = tb * 4 + f
                    w = [(s_w1a, 128), (ve, tp_ev[(4 * tb + 3) * 8 + 7])]
                    if k == 0:
                        w.append((ae, n_ev))
                    if k >= 2:
                        w.append((ae, h_ev[k - 2]))
                    th = None
                    for i in range(8):
                        th = MM(psB[:, (k % 2) * 512:(k % 2) * 512 + 512],
                                kvr[:, i * 512 + f * 128: i * 512 + (f + 1) * 128],
                                xTc[i][:, tb * 512:(tb + 1) * 512],
                                start=(i == 0), stop=(i == 7), waits=w)
                        w = ()
                    h_ev[k] = kb.op("scalar",
                                    lambda e,
                                    o=pool[:, 8192 + (tb % 2) * 2048 + f * 512:
                                           8192 + (tb % 2) * 2048 + (f + 1) * 512],
                                    p=psB[:, (k % 2) * 512:(k % 2) * 512 + 512],
                                    b=c_sb[:, f:f + 1]:
                                    e.activation(out=o, in_=p, func=ACT.Gelu,
                                                 bias=b, scale=1.0),
                                    waits=[(pe, th), (ve, t_c)], inc=ae,
                                    r=[psB[:, (k % 2) * 512:(k % 2) * 512 + 512],
                                       c_sb[:, f:f + 1]],
                                    w=[pool[:, 8192 + (tb % 2) * 2048 + f * 512:
                                            8192 + (tb % 2) * 2048 + (f + 1) * 512]])

            def emit_logits(j):
                tb = j // 4
                reg = psD[0:128, (j % 8) * 64:(j % 8) * 64 + 64]
                w = [(ae, h_ev[tb * 4 + 3]), (s_w2, 64)]
                if j == 0:
                    w.append((ve, m_last_ev))
                if j >= 1:
                    w.append((ve, lcp[j - 1]))
                for f in range(4):
                    MM(reg,
                       pool[:, 8192 + (tb % 2) * 2048 + f * 512 + (j % 4) * 128:
                            8192 + (tb % 2) * 2048 + f * 512 + (j % 4) * 128 + 128],
                       w2_sb[:, f * 64:(f + 1) * 64],
                       start=(f == 0), stop=False, waits=w)
                    w = ()
                lg_t[j] = MM(reg, ones_r[0:1, 0:128], rows[0:1, 3072:3136],
                             start=False, stop=True)

            def emit_chain(j):
                reg = psD[0:128, (j % 8) * 64:(j % 8) * 64 + 64]
                b = (j % 2) * 224
                wv = [(pe, lg_t[j])]
                if j == 0:
                    wv.append((ve, t_ab))
                if j >= 2:
                    wv.append((ae, ee[j - 2]))
                lcp[j] = CP("vector", topk[:, b:b + 64], reg, waits=wv, inc=ve)
                t_m8 = kb.op("vector",
                             lambda e, b=b: e.max(out=topk[:, b + 192:b + 200],
                                                  in_=topk[:, b:b + 64]), inc=vsm,
                             r=[topk[:, b:b + 64]], w=[topk[:, b + 192:b + 200]])
                TS("vector", topk[:, b + 128:b + 192], topk[:, b:b + 64],
                   topk[:, b + 199:b + 200], None, ALU.is_ge, waits=[(vsm, t_m8)])
                t_nm = TS("vector", topk[:, b + 200:b + 201], topk[:, b + 192:b + 193],
                          a_b[:, 0:1], -1.0, ALU.mult, ALU.mult, inc=ve)
                ee[j] = kb.op("scalar",
                              lambda e, b=b: e.activation(out=topk[:, b + 64:b + 128],
                                                          in_=topk[:, b:b + 64],
                                                          func=ACT.Exp,
                                                          bias=topk[:, b + 200:b + 201],
                                                          scale=a_b[:, 0:1]),
                              waits=[(ve, t_nm)], inc=ae,
                              r=[topk[:, b:b + 64], topk[:, b + 200:b + 201],
                                 a_b[:, 0:1]],
                              w=[topk[:, b + 64:b + 128]])
                TT("vector", topk[:, b + 128:b + 192], topk[:, b + 64:b + 128],
                   topk[:, b + 128:b + 192], ALU.mult, waits=[(ae, ee[j])])
                t_red = RED(topk[:, b + 201:b + 202], topk[:, b + 128:b + 192],
                            ALU.add, inc=vsm)
                t_rc = RCP(topk[:, b + 202:b + 203], topk[:, b + 201:b + 202],
                           waits=[(vsm, t_red)], inc=vsm)
                alpha_done[j] = TS("vector", alpha_bf[:, j * 64:(j + 1) * 64],
                                   topk[:, b + 128:b + 192], topk[:, b + 202:b + 203],
                                   None, ALU.mult, waits=[(vsm, t_rc)], inc=ve)

            def emit_alsi(j):
                nonlocal si_done
                ta = TRN(psT[0:64, (j % 4) * 128:(j % 4) * 128 + 128],
                         alpha_bf[:, j * 64:(j + 1) * 64],
                         waits=[(ve, alpha_done[j])])
                alT_ev[j] = CPT(alphaT[0:64, j * 128:(j + 1) * 128],
                                psT[0:64, (j % 4) * 128:(j % 4) * 128 + 128],
                                waits=[(pe, ta)], inc=ve)
                w = [(ve, alpha_done[j])]
                if j == 0:
                    w.append((ve, gh_ev[2]))
                for h in range(2):
                    si_done = MM(psC[0:64, h * 512:(h + 1) * 512],
                                 alpha_bf[:, j * 64:(j + 1) * 64],
                                 x1bf[:, j * 1024 + h * 512: j * 1024 + h * 512 + 512],
                                 start=(j == 0), stop=(j == NT - 1), waits=w)
                    w = ()

            for t in range(NT):
                tk = None
                for h in range(2):
                    w = []
                    if t == 0 and h == 0:
                        w = [(ae, n_ev), (ve, gh_ev[0])]
                    if t >= 1:
                        w = list(w) + [(ve, tt_add[t - 1][h])]
                    for i in range(8):
                        tk = MM(psA[:, h * 512:(h + 1) * 512],
                                xTc[i][:, t * 128:(t + 1) * 128],
                                pool[:, i * 1024 + h * 512: i * 1024 + h * 512 + 512],
                                start=(i == 0), stop=(i == 7), waits=w)
                        w = ()
                    if h == 0:
                        att_h0[t] = tk
                att_done[t] = tk
                xs = fpA[:, (t % 2) * 1024:(t % 2) * 1024 + 1024]
                t0 = TT("vector", xs[:, 0:512], psA[:, 0:512], xs[:, 0:512], ALU.add,
                        waits=[(pe, att_h0[t]), (s_x[t % 2], 16 * (t // 2 + 1))],
                        inc=ve)
                BNS(bnt[:, 0:6], xs[:, 0:512])
                t1 = TT("vector", xs[:, 512:1024], psA[:, 512:1024], xs[:, 512:1024],
                        ALU.add, waits=[(pe, att_done[t])], inc=ve)
                tt_add[t] = (t0, t1)
                BNS(bnt[:, 6:12], xs[:, 512:1024])
                t_bn = BNA(stats[:, 0:2], bnt[:, 0:12], inc=ve)
                t_sq = AV(stats[:, 2:3], stats[:, 1:2], ACT.Sqrt, bias=cst[:, 0:1],
                          scale=1.0, waits=[(ve, t_bn)])
                t_rc = RCP(stats[:, 3:4], stats[:, 2:3], waits=[(ae, t_sq)], inc=vsm)
                x1_done[t] = TS("vector", x1bf[:, t * 1024:(t + 1) * 1024], xs,
                                stats[:, 0:1], stats[:, 3:4], ALU.subtract, ALU.mult,
                                waits=[(vsm, t_rc), (pe, gh_done)], inc=ve)
                y_done[t] = x1_done[t]
                if t + 2 < NT:
                    load_x(t + 2)
                if t >= 1:
                    emit_transposes(t - 1)
                if t >= 5 and (t - 5) % 4 == 0:
                    emit_h((t - 5) // 4)
                if t >= 6:
                    emit_logits(t - 6)
                    emit_chain(t - 6)
                if t >= 7:
                    emit_alsi(t - 7)
            emit_transposes(NT - 1)
            emit_h(3)
            for j in range(10, NT):
                emit_logits(j)
                emit_chain(j)
                emit_alsi(j - 1)
            emit_alsi(NT - 1)
            lg_done = lg_t[NT - 1]

            for i in range(8):
                GD(pool[:, 4352 + i * 256: 4352 + (i + 1) * 256],
                   wqs_d[i * 128:(i + 1) * 128, :], s_wqs,
                   waits=[(pe, att_done[NT - 1])])
            for i in range(8):
                GD(pool[:, 8192 + i * 1024: 8192 + (i + 1) * 1024],
                   wof_d[i * 128:(i + 1) * 128, :], s_wof, waits=[(pe, lg_done)])
            t_den = RED(si_sb[0:64, 1024:1025], alphaT[0:64, :], ALU.add,
                        waits=[(ve, alT_ev[NT - 1])], inc=ve)
            for i in range(8):
                GD(pool[:, i * 256:(i + 1) * 256],
                   wks_d[i * 128:(i + 1) * 128, :], s_wks, waits=[(pe, lg_done)])

            if _PH < 67:
                raise _Cut()
            si_ev = CP("vector", si_sb[0:64, 0:1024], psC[0:64, 0:1024],
                       waits=[(pe, si_done)], inc=ve)
            kb.op("gpsimd", lambda e: e.dma_start(out=si_bi, in_=si_sb[0:64, 0:1040]),
                  waits=[(ve, si_ev), (ve, t_den)], inc=s_a2o, inck=16)
            t_cc2 = kb.op("gpsimd",
                          lambda e: e.collective_compute("AllReduce", ALU.add,
                                                         replica_groups=RG,
                                                         ins=[si_bi.opt()],
                                                         outs=[si_bo.opt()]),
                          waits=[(s_a2o, 16)], inc=cc)
            kb.op("gpsimd", lambda e: e.dma_start(out=si_sb[0:64, 0:1040], in_=si_bo),
                  waits=[(cc, t_cc2)], inc=s_a2i, inck=16)

            if _PH == 67 and _KSUB < 1:
                raise _Cut()
            t_dn = TS("vector", si_sb[0:64, 1025:1026], si_sb[0:64, 1024:1025], 1e-8,
                      None, ALU.add, waits=[(s_a2i, 16)])
            t_dr = RCP(si_sb[0:64, 1026:1027], si_sb[0:64, 1025:1026], inc=vsm)
            t_sin = TS("vector", sin_bf[0:64, :], si_sb[0:64, 0:1024],
                       si_sb[0:64, 1026:1027],
                       None, ALU.mult, waits=[(vsm, t_dr)], inc=ve)

            if _PH == 67 and _KSUB < 2:
                raise _Cut()
            def transpose_64x1024(src_bf, dst, first_wait):
                evs = [None] * 8
                last_trn = None
                for c in range(8):
                    w = []
                    if c == 0:
                        w.append(first_wait)
                    if c >= 4:
                        w.append((ve, evs[c - 4]))
                    last_trn = TRN(psT[0:128, (c % 4) * 128:(c % 4) * 128 + 128],
                                   src_bf[0:128, c * 128:(c + 1) * 128], waits=w)
                    evs[c] = CPT(dst[:, c * 64:(c + 1) * 64],
                                 psT[0:128, (c % 4) * 128:(c % 4) * 128 + 64],
                                 waits=[(pe, last_trn)], inc=ve)
                return last_trn, evs[7]

            _, siT_ev = transpose_64x1024(sin_bf, siT_bf, (ve, t_sin))

            if _PH == 67 and _KSUB < 3:
                raise _Cut()
            giA = None
            for c in range(8):
                for col in range(0, 1536, 512):
                    g = col // 1024
                    h = (col % 1024) // 512
                    w = [(s_wih0, 128)]
                    if c == 0 and col == 0:
                        w += [(ve, siT_ev), (pe, si_done), (ae, h_ev[15])]
                    giA = MM(gh_ps[g][0:64, h * 512:(h + 1) * 512],
                             siT_bf[:, c * 64:(c + 1) * 64],
                             wihb[:, c * 1536 + col: c * 1536 + col + 512],
                             start=(c == 0), stop=False,
                             fin=(c == 7 and col == 1024), waits=w)
            for c in range(8):
                GD(wihb[:, c * 1536:(c + 1) * 1536],
                   wihT_d[c * 128:(c + 1) * 128, 1536:3072], s_wih1,
                   waits=[(pe, giA)])
            giB = None
            for c in range(8):
                for col in range(0, 1536, 512):
                    gcol = 1536 + col
                    g = gcol // 1024
                    h = (gcol % 1024) // 512
                    giB = MM(gh_ps[g][0:64, h * 512:(h + 1) * 512],
                             siT_bf[:, c * 64:(c + 1) * 64],
                             wihb[:, c * 1536 + col: c * 1536 + col + 512],
                             start=(c == 0), stop=False, waits=[(s_wih1, 128)])
            gi_done = None
            for g in range(3):
                for h in range(2):
                    gi_done = MM(gh_ps[g][0:64, h * 512:(h + 1) * 512],
                                 ones_r[0:1, 0:64],
                                 rows[0:1, g * 1024 + h * 512: g * 1024 + h * 512 + 512],
                                 start=False, stop=True,
                                 waits=[(s_bih, 16), SML, (pe, giB)])

            for i in range(8):
                GD(wihb[:, i * 1024:(i + 1) * 1024],
                   wvs_d[i * 128:(i + 1) * 128, :], s_wvs, waits=[(pe, gi_done)])
            for i in range(8):
                GD(wihb[:, 8192 + i * 512: 8192 + (i + 1) * 512],
                   wvp_d[i * 128:(i + 1) * 128, 0:512], s_wvpA,
                   waits=[(pe, gi_done)])
            GD(rows[0:1, 0:1024], bvpr_d, s_bvp)

            if _PH == 67 and _KSUB < 4:
                raise _Cut()
            t_r0 = TT("vector", g_r[:], psA[0:64, 0:1024], gh_bf[:, 0:1024], ALU.add,
                      waits=[(pe, gi_done), (ve, gh_done_v)], inc=ve)
            t_rg = AV(g_r[:], g_r[:], ACT.Sigmoid, waits=[(ve, t_r0)])
            t_z0 = TT("vector", g_z[:], psB[0:64, 0:1024], gh_bf[:, 1024:2048], ALU.add,
                      inc=ve)
            t_zg = AV(g_z[:], g_z[:], ACT.Sigmoid, waits=[(ve, t_z0)])
            TT("vector", g_n[:], g_r[:], gh_bf[:, 2048:3072], ALU.mult,
               waits=[(ae, t_rg)])
            t_n1 = TT("vector", g_n[:], psC[0:64, 0:1024], g_n[:], ALU.add, inc=ve)
            t_ng = AV(g_n[:], g_n[:], ACT.Tanh, waits=[(ve, t_n1)])
            TT("vector", sin_bf[0:64, :], sli_bf[:], g_n[:], ALU.subtract,
               waits=[(ae, t_ng), (ae, t_zg), SML])
            TT("vector", sin_bf[0:64, :], g_z[:], sin_bf[0:64, :], ALU.mult)
            t_s2 = TT("vector", snew_f[:], sin_bf[0:64, :], g_n[:], ALU.add, inc=ve)
            t_sbf = CP("vector", snew_bf[0:64, :], snew_f[:], inc=ve)
            snT_trn, snT_ev = transpose_64x1024(snew_bf, snewT, (ve, t_sbf))

            if _PH == 67 and _KSUB < 5:
                raise _Cut()
            qk_ev = {}
            qk_prev = None
            for which in range(2):
                wbase = 4352 if which == 0 else 0
                brow = 3136 if which == 0 else 3392
                wtok = (s_wqs, 128) if which == 0 else (s_wks, 128)
                for e2 in range(2):
                    reg = psD[0:128, (which * 2 + e2) * 64:(which * 2 + e2) * 64 + 64]
                    w = [(ve, snT_ev), wtok, SML]
                    if qk_prev is not None:
                        w.append((ve, qk_prev))
                    tq = None
                    for i in range(8):
                        tq = MM(reg, pool[:, wbase + i * 256 + e2 * 128:
                                          wbase + i * 256 + e2 * 128 + 128],
                                snewT[:, i * 64:(i + 1) * 64],
                                start=(i == 0), stop=False, waits=w)
                        w = ()
                    tq = MM(reg, rows[0:1, brow + e2 * 128: brow + e2 * 128 + 128],
                            ones_r[0:1, 0:64], start=False, stop=True)
                    dst = qT_bf if which == 0 else kT_bf
                    qk_ev[(which, e2)] = CP("vector", dst[:, e2 * 64:(e2 + 1) * 64], reg,
                                            waits=[(pe, tq)], inc=ve)
                    qk_prev = qk_ev[(which, e2)]
            tv = None
            if _PH == 67 and _KSUB < 6:
                raise _Cut()
            w = [(s_wvs, 128), (ve, snT_ev), (ve, t_n1), SML]
            for i in range(8):
                for h in range(2):
                    tv = MM(psC[0:64, h * 512:(h + 1) * 512],
                            snewT[:, i * 64:(i + 1) * 64],
                            wihb[:, i * 1024 + h * 512: i * 1024 + h * 512 + 512],
                            start=(i == 0), stop=False, waits=w)
                    w = ()
            for h in range(2):
                tv = MM(psC[0:64, h * 512:(h + 1) * 512], ones_r[0:1, 0:64],
                        rows[0:1, 3648 + h * 512: 3648 + h * 512 + 512],
                        start=False, stop=True)
            v_ev = CP("vector", v_bf[:], psC[0:64, 0:1024], waits=[(pe, tv)], inc=ve)

            for i in range(8):
                GD(wihb[:, i * 512:(i + 1) * 512],
                   wvp_d[i * 128:(i + 1) * 128, 512:1024], s_wvpB,
                   waits=[(pe, tv)])

            if _PH == 67 and _KSUB < 7:
                raise _Cut()
            tl = MM(psD[0:64, 256:320], qT_bf[:, 0:64], kT_bf[:, 0:64], True, False,
                    waits=[(ve, qk_ev[(0, 0)]), (ve, qk_ev[(0, 1)]),
                           (ve, qk_ev[(1, 0)]), (ve, qk_ev[(1, 1)])])
            tl = MM(psD[0:64, 256:320], qT_bf[:, 64:128], kT_bf[:, 64:128], False, True)
            RED(zrow[:, 0:1], psD[0:64, 256:320], ALU.max, waits=[(pe, tl)])
            t_mxs = TS("vector", zrow[:, 1:2], zrow[:, 0:1], -1.0 / 16.0, None, ALU.mult,
                       inc=vsm)
            t_E = kb.op("scalar",
                        lambda e: e.activation(out=E_bf[0:64, :],
                                               in_=psD[0:64, 256:320],
                                               func=ACT.Exp, bias=zrow[0:64, 1:2],
                                               scale=cst[0:64, 2:3]),
                        waits=[(vsm, t_mxs)], inc=ae,
                        r=[psD[0:64, 256:320], zrow[0:64, 1:2]],
                        w=[E_bf[0:64, :]])
            tE = TRN(psT[0:64, 384:512], E_bf[0:128, :], waits=[(ae, t_E)])
            ET_ev = CPT(ET_bf[:], psT[0:64, 384:448], waits=[(pe, tE)], inc=ve)
            tZ = MM(psD[0:64, 320:321], ET_bf[:], ones_c[0:64, :], True, True,
                    waits=[(ve, ET_ev)])
            tS = None
            w = [(ve, v_ev), (ve, t_r0)]
            for h in range(2):
                tS = MM(psA[0:64, h * 512:(h + 1) * 512], ET_bf[:],
                        v_bf[:, h * 512:(h + 1) * 512], True, True, waits=w)
                w = ()
            CP("vector", zrow[:, 2:3], psD[0:64, 320:321], waits=[(pe, tZ)])
            t_Zr = RCP(zrow[:, 3:4], zrow[:, 2:3], inc=vsm)
            t_ys = STT("vector", si_sb[0:64, 0:1024], psA[0:64, 0:1024], zrow[0:64, 3:4],
                       snew_f[:], ALU.mult, ALU.add,
                       waits=[(vsm, t_Zr), (pe, tS), (ve, t_s2)], inc=ve)
            BNS(bnt[0:64, 0:6], si_sb[0:64, 0:512], waits=[(ve, t_ys)])
            BNS(bnt[0:64, 6:12], si_sb[0:64, 512:1024])
            t_bns = BNA(stats[0:64, 0:2], bnt[0:64, 0:12], inc=ve)
            t_sqs = AV(stats[0:64, 2:3], stats[0:64, 1:2], ACT.Sqrt, bias=cst[0:64, 0:1], scale=1.0,
                       waits=[(ve, t_bns)])
            t_rcs = RCP(stats[0:64, 3:4], stats[0:64, 2:3], waits=[(ae, t_sqs)], inc=vsm)
            t_yn = TS("vector", snew_bf[0:64, :], si_sb[0:64, 0:1024],
                      stats[0:64, 0:1],
                      stats[0:64, 3:4], ALU.subtract, ALU.mult,
                      waits=[(vsm, t_rcs), (pe, snT_trn)], inc=ve)

            if _PH == 67 and _KSUB < 8:
                raise _Cut()
            tb2 = None
            w = [(s_wvpA, 128), (s_wvpB, 128), (s_bvp, 16), (ve, t_z0), SML]
            for i in range(8):
                tb2 = MM(psB[0:1, 0:512], lnb_bf[:, i:i + 1],
                         wihb[:, 8192 + i * 512: 8192 + (i + 1) * 512],
                         start=(i == 0), stop=False, waits=w)
                w = ()
                tb2 = MM(psB[0:1, 512:1024], lnb_bf[:, i:i + 1],
                         wihb[:, i * 512:(i + 1) * 512],
                         start=(i == 0), stop=False)
            tb2 = MM(psB[0:1, 0:512], ones_r[0:1, 0:1], rows[0:1, 0:512],
                     start=False, stop=True)
            tb2 = MM(psB[0:1, 512:1024], ones_r[0:1, 0:1], rows[0:1, 512:1024],
                     start=False, stop=True)
            bvp_ev = CP("vector", bvppr[:], psB[0:1, 0:1024], waits=[(pe, tb2)], inc=ve)
            if _PH == 67 and _KSUB < 9:
                raise _Cut()
            t_wvs = None
            for i in range(8):
                TS("vector", wihb[:, 8192 + i * 512: 8192 + (i + 1) * 512],
                   wihb[:, 8192 + i * 512: 8192 + (i + 1) * 512],
                   lng_sb[:, i:i + 1], None, ALU.mult, waits=[(pe, tb2), SML])
                t_wvs = TS("vector", wihb[:, i * 512:(i + 1) * 512],
                           wihb[:, i * 512:(i + 1) * 512],
                           lng_sb[:, i:i + 1], None, ALU.mult, inc=ve)
            if _PH == 67 and _KSUB < 10:
                raise _Cut()
            ynT_trn, ynT_ev = transpose_64x1024(snew_bf, snewT, (ve, t_yn))
            if _PH == 67 and _KSUB < 11:
                raise _Cut()
            tsv = None
            w = [(ve, ynT_ev), (ve, t_wvs), (ve, t_ys)]
            for i in range(8):
                tsv = MM(psA[0:64, 0:512], snewT[:, i * 64:(i + 1) * 64],
                         wihb[:, 8192 + i * 512: 8192 + (i + 1) * 512],
                         start=(i == 0), stop=False, waits=w)
                w = ()
                tsv = MM(psA[0:64, 512:1024], snewT[:, i * 64:(i + 1) * 64],
                         wihb[:, i * 512:(i + 1) * 512],
                         start=(i == 0), stop=False)
            tsv = MM(psA[0:64, 0:512], ones_r[0:1, 0:64], bvppr[0:1, 0:512],
                     start=False, stop=True, waits=[(ve, bvp_ev)])
            tsv = MM(psA[0:64, 512:1024], ones_r[0:1, 0:64], bvppr[0:1, 512:1024],
                     start=False, stop=True)
            sv_ev = CP("vector", sin_bf[0:64, :], psA[0:64, 0:1024],
                       waits=[(pe, tsv)], inc=ve)
            if _PH == 67 and _KSUB < 12:
                raise _Cut()
            svT_trn, svT_ev = transpose_64x1024(sin_bf, snewT, (ve, sv_ev))
            if _PH == 67 and _KSUB < 13:
                raise _Cut()
            tsp = None
            w = [(ve, svT_ev), (s_wof, 128)]
            for i in range(8):
                for h in range(2):
                    tsp = MM(psA[0:64, h * 512:(h + 1) * 512],
                             snewT[:, i * 64:(i + 1) * 64],
                             pool[:, 8192 + i * 1024 + h * 512:
                                  8192 + i * 1024 + h * 512 + 512],
                             start=(i == 0), stop=(i == 7), waits=w)
                    w = ()
            svp_ev = CP("vector", svpp[0:64, :], psA[0:64, 0:1024],
                        waits=[(pe, tsp)], inc=ve)

            if _PH < 70:
                raise _Cut()
            mix_ps = [psA, psB]
            v_done = [None] * NT
            stok = [None] * NT
            _KROWS = int(os.environ.get("KROWS", "65"))
            _KTILES = int(os.environ.get("KTILES", str(NT)))
            for t in range(NT):
                ps = mix_ps[t % 2]
                w = [(ve, svp_ev), (ve, alT_ev[NT - 1]), (ve, ve_const), SML]
                if t >= 2:
                    w.append((ve, v_done[t - 2]))
                tm = None
                if t >= _KTILES:
                    v_done[t] = ve.n
                    stok[t] = 0
                    continue
                for h in range(2):
                    tm = MM(ps[0:128, h * 512:(h + 1) * 512],
                            alphaT[0:_KROWS, t * 128:(t + 1) * 128],
                            svpp[0:_KROWS, h * 512:(h + 1) * 512],
                            start=True, stop=True, waits=w)
                    w = ()
                vs_ = fpA[:, (t % 2) * 1024:(t % 2) * 1024 + 1024]
                w = [(pe, tm)]
                if t >= 2:
                    w.append((s_st[t % 2], 16 * (t // 2)))
                if _PH == 70 and _KSUB < 2:
                    v_done[t] = ve.n
                    stok[t] = 0
                    continue
                t_vadd = TT("vector", vs_, ps[0:128, 0:1024],
                            x1bf[:, t * 1024:(t + 1) * 1024],
                            ALU.add, waits=w, inc=ve)
                if _PH == 70 and _KSUB < 3:
                    v_done[t] = t_vadd
                    stok[t] = 0
                    continue
                BNS(bnt[:, 0:6], vs_[:, 0:512])
                BNS(bnt[:, 6:12], vs_[:, 512:1024])
                t_bn = BNA(stats[:, 0:2], bnt[:, 0:12], inc=ve)
                t_sq = AV(stats[:, 2:3], stats[:, 1:2], ACT.Sqrt, bias=cst[:, 0:1], scale=1.0,
                          waits=[(ve, t_bn)])
                t_rc = RCP(stats[:, 3:4], stats[:, 2:3], waits=[(ae, t_sq)], inc=vsm)
                if _PH == 70 and _KSUB < 4:
                    v_done[t] = t_vadd
                    stok[t] = 0
                    continue
                if _PH == 70 and _KSUB == 6:
                    v_done[t] = TS("vector", topk[:, 0:64], vs_[:, 0:64],
                                   stats[:, 0:1], stats[:, 3:4],
                                   ALU.subtract, ALU.mult, waits=[(vsm, t_rc)],
                                   inc=ve)
                else:
                    v_done[t] = TS("vector", vs_, vs_, stats[:, 0:1], stats[:, 3:4],
                                   ALU.subtract, ALU.mult, waits=[(vsm, t_rc)], inc=ve)
                if _PH == 70 and _KSUB < 5:
                    stok[t] = 0
                    continue
                stok[t] = kb.op("sync",
                                lambda e, t=t, src=vs_: e.dma_start(
                                    out=out_hbm[t * 128:(t + 1) * 128, :], in_=src),
                                waits=[(ve, v_done[t])], inc=s_st[t % 2], inck=16)
            if not (_PH == 70 and _KSUB < 5):
                kb.op("sync", lambda e: None,
                      waits=[(s_st[0], 16 * 8), (s_st[1], 16 * 8)])
            else:
                kb.op("sync", lambda e: e.dma_start(
                    out=out_hbm[0:128, :], in_=fpA[:, 0:1024]),
                      waits=[(ve, ve.n), (pe, pe.n)], inc=s_st[0], inck=16)
                kb.op("sync", lambda e: None, waits=[(s_st[0], 16)])

            _emit_blocks(kb, nc)
        except _Cut:
            kb.op("sync", lambda e: None,
                  waits=[(pe, pe.n), (ve, ve.n), (ae, ae.n),
                         (vsm, vsm.n), (cc, cc.n)])
            kb.op("sync", lambda e: e.dma_start(
                out=out_hbm[0:128, :], in_=fpA[:, 0:1024]),
                  inc=s_st[0], inck=16)
            kb.op("sync", lambda e: None, waits=[(s_st[0], 16)])
            _emit_blocks(kb, nc)
    return nc


def _emit_blocks(kb, nc):
    with nc.Block() as block:
        block.tensor(kb.runner("tensor"))
        block.vector(kb.runner("vector"))
        block.scalar(kb.runner("scalar"))
        block.gpsimd(kb.runner("gpsimd"))
        block.sync(kb.runner("sync"))





_NC_CACHE = None


def kernel(L0_attn_x, L0_attn_W_q, L0_attn_W_k, L0_attn_W_v, L0_attn_W_o,
           L0_ffn_slot_init, L0_ffn_w1, L0_ffn_b1, L0_ffn_w2, L0_ffn_b2,
           L0_ffn_tau, L0_ffn_Wih, L0_ffn_Whh, L0_ffn_bih, L0_ffn_bhh,
           L0_ffn_wq, L0_ffn_bq, L0_ffn_wk, L0_ffn_bk, L0_ffn_wv, L0_ffn_bv,
           L0_ffn_ln_g, L0_ffn_ln_b, L0_ffn_wvp, L0_ffn_bvp, L0_ffn_wo,
           L0_ffn_bo):
    global _NC_CACHE
    f32 = np.float32
    asc = np.ascontiguousarray

    x = np.asarray(L0_attn_x, dtype=f32)
    B, S, _ = x.shape

    def chunkvec(v, n):
        return asc(np.asarray(v, f32).reshape(n, 128).T)

    common = {
        "wkT": asc(np.asarray(L0_attn_W_k, f32).T),
        "wvT": asc(np.asarray(L0_attn_W_v, f32).T),
        "wq": asc(np.asarray(L0_attn_W_q, f32)),
        "woT": asc(np.asarray(L0_attn_W_o, f32).T),
        "w1a": asc(np.asarray(L0_ffn_w1, f32)[:D]),
        "w1b": asc(np.asarray(L0_ffn_w1, f32)[D:]),
        "w2": asc(np.asarray(L0_ffn_w2, f32)),
        "b1c": chunkvec(L0_ffn_b1, 4),
        "b2r": asc(np.asarray(L0_ffn_b2, f32).reshape(1, 64)),
        "tau": asc(np.asarray(L0_ffn_tau, f32).reshape(1, 1)),
        "wihT": asc(np.asarray(L0_ffn_Wih, f32).T),
        "whhT": asc(np.asarray(L0_ffn_Whh, f32).T),
        "bihr": asc(np.asarray(L0_ffn_bih, f32).reshape(1, 3072)),
        "bhhr": asc(np.asarray(L0_ffn_bhh, f32).reshape(1, 3072)),
        "sli": asc(np.asarray(L0_ffn_slot_init, f32)),
        "sliTp": asc(np.asarray(L0_ffn_slot_init, f32).T
                     .reshape(8, 128, 64).transpose(1, 0, 2).reshape(128, 512)),
        "wqs": asc(np.asarray(L0_ffn_wq, f32)),
        "bqr": asc(np.asarray(L0_ffn_bq, f32).reshape(1, 256)),
        "wks": asc(np.asarray(L0_ffn_wk, f32)),
        "bkr": asc(np.asarray(L0_ffn_bk, f32).reshape(1, 256)),
        "wvs": asc(np.asarray(L0_ffn_wv, f32)),
        "bvr": asc(np.asarray(L0_ffn_bv, f32).reshape(1, D)),
        "lngp": chunkvec(L0_ffn_ln_g, 8),
        "lnbp": chunkvec(L0_ffn_ln_b, 8),
        "wvp": asc(np.asarray(L0_ffn_wvp, f32)),
        "bvpr": asc(np.asarray(L0_ffn_bvp, f32).reshape(1, D)),
        "wof": asc(np.asarray(L0_ffn_wo, f32)),
        "bor": asc(np.asarray(L0_ffn_bo, f32).reshape(1, D)),
        "idn": np.eye(128, dtype=f32),
    }

    in_maps = []
    for core in range(8):
        b = core // 4
        s0 = (core % 4) * T
        shard = asc(x[b, s0:s0 + T, :])
        m = dict(common)
        m["x_tm"] = shard
        m["xT"] = asc(shard.T)
        in_maps.append(m)

    if _NC_CACHE is None:
        _NC_CACHE = build_program()
    res = run_bass_kernel_spmd(_NC_CACHE, in_maps, core_ids=list(range(8)))
    out = np.empty((B, S, D), dtype=f32)
    for core in range(8):
        b = core // 4
        s0 = (core % 4) * T
        out[b, s0:s0 + T, :] = res.results[core]["out"]
    return out
